# version 16 (final)
# baseline (speedup 1.0000x reference)
import contextlib
import math
import numpy as np
import concourse.bass as bass
import concourse.mybir as mybir
from concourse.bass_utils import run_bass_kernel_spmd

F32 = mybir.dt.float32
BF16 = mybir.dt.bfloat16
AF = mybir.ActivationFunctionType
ALU = mybir.AluOpType

B, T, D, SDIM = 8, 4096, 2048, 128
INTER = 8192
P = 512
NT = T // P
NCD = D // 128
NMI = INTER // 128
EPS = 1e-5
C_SU1 = 1.0 / (64.0 * math.sqrt(2048.0))
C_UP = 64.0
C_DN = 1.0 / math.sqrt(8192.0)
DBG = False
DBG2 = False


class Sem:
    def __init__(self, h):
        self.h = h
        self.n = 0

    def bump(self, k=1):
        self.n += k
        return self.n


def build_nc():
    nc = bass.Bass("TRN2", target_bir_lowering=False)

    xT = nc.dram_tensor("xT", [D, T], F32, kind="ExternalInput").ap()
    w_in = nc.dram_tensor("w_in", [D, 3 * SDIM], F32, kind="ExternalInput").ap()
    w_out = nc.dram_tensor("w_out", [SDIM, D], F32, kind="ExternalInput").ap()
    w_skip = nc.dram_tensor("w_skip", [D, D], F32, kind="ExternalInput").ap()
    w_up = nc.dram_tensor("w_up", [INTER, D], F32, kind="ExternalInput").ap()
    w_gate = nc.dram_tensor("w_gate", [INTER, D], F32, kind="ExternalInput").ap()
    w_down = nc.dram_tensor("w_down", [D, INTER], F32, kind="ExternalInput").ap()
    v_gamma = nc.dram_tensor("v_gamma", [D], F32, kind="ExternalInput").ap()
    v_beta = nc.dram_tensor("v_beta", [D], F32, kind="ExternalInput").ap()
    v_bin = nc.dram_tensor("v_bin", [3 * SDIM], F32, kind="ExternalInput").ap()
    v_bout = nc.dram_tensor("v_bout", [D], F32, kind="ExternalInput").ap()
    v_bskip = nc.dram_tensor("v_bskip", [D], F32, kind="ExternalInput").ap()
    v_suug = nc.dram_tensor("v_suug", [D], F32, kind="ExternalInput").ap()
    v_sudp = nc.dram_tensor("v_sudp", [INTER], F32, kind="ExternalInput").ap()
    v_wsug = nc.dram_tensor("v_wsug", [2 * INTER], F32, kind="ExternalInput").ap()
    v_wsdp = nc.dram_tensor("v_wsdp", [D], F32, kind="ExternalInput").ap()
    oT = nc.dram_tensor("oT", [D, T], F32, kind="ExternalOutput").ap()

    dbg = {}
    if DBG2:
        for nm, shape in [("d_skall", [16, 128, 16, 128]), ("d_upall", [64, 128, 16, 128]),
                          ("d_gtall", [64, 128, 16, 128]), ("d_dnall", [16, 128, 64, 128])]:
            dbg[nm] = nc.dram_tensor(nm, shape, BF16, kind="ExternalOutput").ap()
    if DBG:
        for nm, shape, dt in [
            ("d_xin", [128, 16, 512], BF16), ("d_u", [128, 16, 512], BF16),
            ("d_al", [128, 4, 512], F32), ("d_ao", [128, 16, 512], F32),
            ("d_x3", [128, 16, 512], F32), ("d_x1", [128, 16, 512], BF16),
            ("d_z", [128, 64, 512], BF16), ("d_v", [128, 64, 512], BF16),
            ("d_fo", [128, 16, 512], F32), ("d_st1", [128, 3, 512], F32),
            ("d_h128", [128, 128], BF16), ("d_gam", [128, 16], F32),
            ("d_ups", [128, 64], F32), ("d_su1", [128, 16], F32),
            ("d_bcb", [128, 16], F32), ("d_up0", [128, 16, 128], BF16),
            ("d_dn0", [128, 64, 128], BF16), ("d_sk0", [128, 16, 128], BF16), ("d_ups2", [128, 64], F32), ("d_upv", [128, 1024], F32), ("d_z2", [128, 64, 512], BF16), ("d_car", [128, 8], F32), ("d_h511", [128, 8], F32),
        ]:
            dbg[nm] = nc.dram_tensor(nm, shape, dt, kind="ExternalOutput").ap()
    win_s = nc.dram_tensor("win_s", [128, 16, 384], BF16, kind="Internal").ap()
    wout_s = nc.dram_tensor("wout_s", [128, 2048], BF16, kind="Internal").ap()
    skip_s = nc.dram_tensor("skip_s", [16, 128, 16, 128], BF16, kind="Internal").ap()
    up_s = nc.dram_tensor("up_s", [64, 128, 16, 128], BF16, kind="Internal").ap()
    gate_s = nc.dram_tensor("gate_s", [64, 128, 16, 128], BF16, kind="Internal").ap()
    down_s = nc.dram_tensor("down_s", [16, 128, 64, 128], BF16, kind="Internal").ap()

    with contextlib.ExitStack() as es:
        E = es.enter_context
        X = E(nc.sbuf_tensor("X", [128, NCD, P], F32))
        A = E(nc.sbuf_tensor("A", [128, 8192], F32))
        Bf = E(nc.sbuf_tensor("Bf", [128, 8192], F32))
        STT = E(nc.sbuf_tensor("STT", [128, 3, 512], F32))
        STG = E(nc.sbuf_tensor("STG", [128, 512], F32))
        MSC = E(nc.sbuf_tensor("MSC", [128, 16], F32))
        ONE = E(nc.sbuf_tensor("ONE", [128, 128], F32))
        GAM = E(nc.sbuf_tensor("GAM", [128, 16], F32))
        BET = E(nc.sbuf_tensor("BET", [128, 16], F32))
        BIN = E(nc.sbuf_tensor("BIN", [128, 3], F32))
        NBIN = E(nc.sbuf_tensor("NBIN", [128, 3], F32))
        BCB = E(nc.sbuf_tensor("BCB", [128, 16], F32))
        SU1 = E(nc.sbuf_tensor("SU1", [128, 16], F32))
        UPS = E(nc.sbuf_tensor("UPS", [128, 64], F32))
        GTS = E(nc.sbuf_tensor("GTS", [128, 64], F32))
        DNS = E(nc.sbuf_tensor("DNS", [128, 16], F32))
        T16 = E(nc.sbuf_tensor("T16", [128, 16], F32))
        T64 = E(nc.sbuf_tensor("T64", [128, 64], F32))
        Z = E(nc.sbuf_tensor("Z", [128, 32768], BF16))
        XB = E(nc.sbuf_tensor("XB", [128, NCD, P], BF16))
        WS = E(nc.sbuf_tensor("WS", [128, 8192], BF16))
        H128 = E(nc.sbuf_tensor("H128", [128, 128], BF16))
        IDB = E(nc.sbuf_tensor("IDB", [128, 128], BF16))
        ONEB = E(nc.sbuf_tensor("ONEB", [128, 1], BF16))
        p_a = E(nc.psum_tensor("p_a", [128, 512], F32))
        p_b = E(nc.psum_tensor("p_b", [128, 512], F32))
        p_c = E(nc.psum_tensor("p_c", [128, 512], F32))
        p_d = E(nc.psum_tensor("p_d", [128, 512], F32))
        p_s1 = E(nc.psum_tensor("p_s1", [128, 512], F32))
        p_s2 = E(nc.psum_tensor("p_s2", [128, 512], F32))
        sL = Sem(E(nc.semaphore("sL")))
        sC = Sem(E(nc.semaphore("sC")))
        sP = Sem(E(nc.semaphore("sP")))
        sV = Sem(E(nc.semaphore("sV")))
        sA = Sem(E(nc.semaphore("sA")))
        sS = Sem(E(nc.semaphore("sS")))
        sM = Sem(E(nc.semaphore("sM")))
        sN = Sem(E(nc.semaphore("sN")))
        block = E(nc.Block())

        nc.const_aps.register(F32, EPS, MSC[:, 1:2])

        pts4 = (p_a, p_b, p_c, p_d)
        Af = A[:]
        A3 = A.rearrange("p (c t) -> p c t", c=16)
        B3 = Bf.rearrange("p (c t) -> p c t", c=16)
        T1 = A[:, 0:4096].rearrange("p (c t) -> p c t", c=64)
        T2 = A[:, 4096:8192].rearrange("p (c t) -> p c t", c=64)
        AL = A[:, 0:2048].rearrange("p (s t) -> p s t", s=4)
        Z3 = Z.rearrange("p (c t) -> p c t", c=64)
        WIN = Z[:, 0:6144].rearrange("p (c m) -> p c m", c=16)
        WOU = Z[:, 6144:8192]
        SKP = Z[:, 8192:12288].rearrange("p (b q) -> p b q", b=2)
        UFL = Z[:, 12288:20480].rearrange("p (c t) -> p c t", c=16)
        YB = Z[:, 20480:20992]
        MSLAB = Z
        ASM = XB.rearrange("p c t -> p (c t)")
        Xfl = X.rearrange("p c t -> p (c t)")
        ROW = Xfl[0:1, :]
        IO2 = Bf[:, 0:128]
        IDF = Bf[:, 128:256]
        WSu = (WS[:, 0:2048], WS[:, 2048:4096])
        WSg = (WS[:, 4096:6144], WS[:, 6144:8192])
        WSd = (WS[:, 0:4096], WS[:, 4096:8192])
        M_ = STT[:, 0, :]
        R_ = STT[:, 1, :]
        V_ = STT[:, 2, :]
        xT3 = xT.rearrange("(c p) t -> p c t", p=128)
        oT3 = oT.rearrange("(c p) t -> p c t", p=128)
        dbg_last = [0]

        def dbg_tap(name, src_ap, wsem=None, wval=None):
            if not DBG:
                return
            @block.sync
            def _(s, name=name, src_ap=src_ap, wsem=wsem, wval=wval):
                if wsem is not None:
                    s.wait_ge(wsem.h, wval)
                s.dma_start(out=dbg[name], in_=src_ap).then_inc(sS.h, 16)
                sS.bump(16)
            dbg_last[0] = sS.n

        def dbg_gate(eng):
            if DBG:
                eng.wait_ge(sS.h, dbg_last[0])

        @block.gpsimd
        def _(g):
            g.memset(MSC[:, 0:1], 0.0)
            g.memset(MSC[:, 1:2], EPS)
            g.memset(ONE[:], 1.0)
            g.iota(MSC[:, 2:3], pattern=[[0, 1]], base=0, channel_multiplier=1,
                   allow_small_or_imprecise_dtypes=True)
            g.iota(IO2, pattern=[[1, 128]], base=0, channel_multiplier=0,
                   allow_small_or_imprecise_dtypes=True)
            g.tensor_scalar(out=IDF, in0=IO2, scalar1=MSC[:, 2:3], scalar2=None, op0=ALU.is_equal)
            g.drain().then_inc(sC.h, 1)
            sC.bump(1)

        mark_iota = sC.n

        @block.scalar
        def _(sc):
            sc.wait_ge(sC.h, mark_iota)
            sc.copy(IDB[:], IDF)
            sc.copy(ONEB[:], ONE[:, 0:1])
            sc.drain().then_inc(sA.h, 1)
            sA.bump()

        mark_idb = sA.n

        @block.vector
        def _(v):
            v.wait_ge(sC.h, mark_iota)
            v.wait_ge(sA.h, mark_idb)
            src, dst = IDF, STG[:, 0:128]
            for h in (1, 2, 4, 8, 16, 32, 64):
                ng = 128 // (2 * h)
                si = src.rearrange("p (g two i) -> p g two i", g=ng, two=2)
                di = dst.rearrange("p (g two i) -> p g two i", g=ng, two=2)
                v.tensor_tensor(out=di[:, :, 0], in0=si[:, :, 0], in1=si[:, :, 1], op=ALU.add)
                v.tensor_tensor(out=di[:, :, 1], in0=si[:, :, 0], in1=si[:, :, 1], op=ALU.subtract)
                src, dst = dst, src
            v.drain().then_inc(sV.h, 1)
            sV.bump()

        mark_h128_dve = sV.n
        h128_src = STG[:, 0:128]

        @block.scalar
        def _(sc):
            sc.wait_ge(sV.h, mark_h128_dve)
            sc.copy(H128[:], h128_src)
            sc.drain().then_inc(sA.h, 1)
            sA.bump()

        mark_setup_act = sA.n
        mark_setup = sV.n

        waves = [
            [(v_gamma, D, GAM), (v_beta, D, BET), (v_bin, 3 * SDIM, BIN), (v_bout, D, BCB)],
            [(v_bskip, D, T16), (v_suug, D, SU1), (v_wsdp, D, DNS)],
            [(v_sudp, INTER, T64)],
            [(v_wsug[0:INTER], INTER, UPS)],
            [(v_wsug[INTER:2 * INTER], INTER, GTS)],
        ]
        for wv in waves:
            offs = []
            o = 0
            for dv, ln, dst in wv:
                offs.append(o)
                o += ln
            prev_evac = sA.n

            @block.sync
            def _(s, wv=wv, offs=offs, pe=prev_evac):
                s.wait_ge(sA.h, pe)
                for (dv, ln, dst), off in zip(wv, offs):
                    s.dma_start(out=ROW[:, off:off + ln], in_=dv.unsqueeze(0)).then_inc(sL.h, 16)
                    sL.bump(16)

            mark_row = sL.n

            @block.tensor
            def _(t, wv=wv, offs=offs, mr=mark_row):
                t.wait_ge(sL.h, mr)
                t.wait_ge(sC.h, mark_iota)
                col = 0
                for (dv, ln, dst), off in zip(wv, offs):
                    for i in range(ln // 128):
                        t.matmul(p_s1[:, col:col + 1], ROW[:, off + i * 128:off + (i + 1) * 128],
                                 ONE[0:1, 0:1], start=True, stop=True)
                        col += 1
                t.drain().then_inc(sP.h, 1)
                sP.bump()

            mark_mm = sP.n

            @block.scalar
            def _(sc, wv=wv, mm=mark_mm):
                sc.wait_ge(sP.h, mm)
                col = 0
                for dv, ln, dst in wv:
                    nc_ = ln // 128
                    sc.copy(dst[:, 0:nc_], p_s1[:, col:col + nc_])
                    col += nc_
                sc.drain().then_inc(sA.h, 1)
                sA.bump()

        mark_vecs = sA.n

        @block.vector
        def _(v):
            v.wait_ge(sA.h, mark_vecs)
            v.wait_ge(sC.h, mark_iota)
            v.tensor_tensor(out=BCB[:], in0=BCB[:], in1=T16[:], op=ALU.add)
            v.scalar_tensor_tensor(out=SU1[:], in0=SU1[:], scalar=C_SU1, in1=ONE[:, 0:16],
                                   op0=ALU.mult, op1=ALU.mult)
            v.scalar_tensor_tensor(out=UPS[:], in0=UPS[:], scalar=C_UP, in1=T64[:],
                                   op0=ALU.mult, op1=ALU.mult)
            v.scalar_tensor_tensor(out=GTS[:], in0=GTS[:], scalar=C_UP, in1=ONE[:, 0:64],
                                   op0=ALU.mult, op1=ALU.mult)
            v.scalar_tensor_tensor(out=DNS[:], in0=DNS[:], scalar=C_DN, in1=ONE[:, 0:16],
                                   op0=ALU.mult, op1=ALU.mult)
            v.scalar_tensor_tensor(out=NBIN[:], in0=BIN[:], scalar=-1.0, in1=ONE[:, 0:3],
                                   op0=ALU.mult, op1=ALU.mult)
            v.drain().then_inc(sV.h, 1)
            sV.bump()

        @block.gpsimd
        def _(g):
            g.dma_start(out=WIN[:], in_=w_in.rearrange("(kc p) m -> p kc m", p=128)).then_inc(sC.h, 16)
            sC.bump(16)
            g.dma_start(out=WOU, in_=w_out).then_inc(sC.h, 16)
            sC.bump(16)

        mark_wz_prep = sC.n

        @block.sync
        def _(s):
            s.wait_ge(sC.h, mark_wz_prep)
            s.dma_start(out=win_s, in_=WIN[:]).then_inc(sL.h, 16)
            sL.bump(16)
            s.dma_start(out=wout_s, in_=WOU).then_inc(sL.h, 16)
            sL.bump(16)

        mark_wz_store = sL.n

        w_skip4 = w_skip.rearrange("(kc p) (ms mi) -> ms p kc mi", p=128, mi=128)
        skip_store = {}
        for ms in range(16):
            half = ms % 2
            asmv = ASM[:, half * 2048:(half + 1) * 2048]
            wprev = skip_store.get(ms - 2)

            @block.gpsimd
            def _(g, ms=ms, asmv=asmv, wprev=wprev):
                if wprev is not None:
                    g.wait_ge(sL.h, wprev)
                g.dma_start(out=asmv.rearrange("p (kc mi) -> p kc mi", kc=16),
                            in_=w_skip4[ms]).then_inc(sC.h, 16)
                sC.bump(16)

            mark_cast = sC.n

            @block.sync
            def _(s, ms=ms, asmv=asmv, mcst=mark_cast):
                s.wait_ge(sC.h, mcst)
                s.dma_start(out=skip_s[ms], in_=asmv.rearrange("p (kc mi) -> p kc mi", kc=16)).then_inc(sL.h, 16)
                sL.bump(16)

            skip_store[ms] = sL.n

        prep_hand = {"tp": None, "st": (sL.h, sL.n), "ev": [], "mslL": mark_wz_store,
                     "first_weight": True, "ld": []}

        def prep_transposed(wsrc, scratch, n_ms, n_kc, scl=None):
            w3 = wsrc.rearrange("(ms p) k -> ms p k", p=128)
            kcols = n_kc * 128
            U = 4096
            pair = kcols == 2048
            n_units = (n_ms * kcols) // U
            cast_marks = {}
            tp_marks = {}
            st_marks = {}
            st_last = [prep_hand["st"]]
            first_w = prep_hand["first_weight"]
            prep_hand["first_weight"] = False
            pts_prep = (p_c, p_d, p_a, p_b)
            for i in range(n_units):
                par = i % 2
                fstg = Bf[:, par * U:(par + 1) * U]
                msl = MSLAB[:, par * U:(par + 1) * U]
                if pair:
                    asmv = ASM[:, par * U:(par + 1) * U]
                    srcap = w3[2 * i:2 * i + 2].rearrange("m2 p k -> p m2 k")
                else:
                    ms, hh = i // 2, i % 2
                    asmv = ASM[:, hh * U:(hh + 1) * U]
                    srcap = w3[ms][:, hh * U:(hh + 1) * U]
                w_fstg = cast_marks.get(i - 2, prep_hand.get("cast_last"))
                w_msl = tp_marks.get(i - 2, prep_hand["tp"])
                w_asm = st_marks.get(i - 2, prep_hand["st"])

                if i % 2 == 0:
                    @block.sync
                    def _(s, srcap=srcap, fstg=fstg, wf=w_fstg, gate_idf=(first_w and i < 2)):
                        if wf is not None:
                            s.wait_ge(sV.h, wf)
                        if gate_idf:
                            s.wait_ge(sA.h, mark_setup_act)
                        s.dma_start(out=fstg, in_=srcap).then_inc(sL.h, 16)
                        sL.bump(16)

                    mark_ld = (sL.h, sL.n)
                else:
                    @block.scalar
                    def _(sc, srcap=srcap, fstg=fstg, wf=w_fstg, gate_idf=(first_w and i < 2)):
                        if wf is not None:
                            sc.wait_ge(sV.h, wf)
                        if gate_idf:
                            sc.wait_ge(sA.h, mark_setup_act)
                        sc.dma_start(out=fstg, in_=srcap).then_inc(sM.h, 16)
                        sM.bump(16)

                    mark_ld = (sM.h, sM.n)

                @block.vector
                def _(v, i=i, fstg=fstg, msl=msl, ml=mark_ld, wm=w_msl, first2=(first_w and i < 2)):
                    v.wait_ge(ml[0], ml[1])
                    if wm is not None:
                        v.wait_ge(sP.h, wm)
                    if first2 and prep_hand["mslL"]:
                        v.wait_ge(sL.h, prep_hand["mslL"])
                    if first2:
                        v.wait_ge(sC.h, mark_iota)
                    if scl is None:
                        v.tensor_tensor(out=msl, in0=fstg,
                                        in1=ONE[:, 0:1].broadcast_to([128, 4096]), op=ALU.mult)
                    elif pair:
                        v.tensor_tensor(out=msl[:, 0:2048], in0=fstg[:, 0:2048],
                                        in1=scl[:, 2 * i:2 * i + 1].broadcast_to([128, 2048]), op=ALU.mult)
                        v.tensor_tensor(out=msl[:, 2048:4096], in0=fstg[:, 2048:4096],
                                        in1=scl[:, 2 * i + 1:2 * i + 2].broadcast_to([128, 2048]), op=ALU.mult)
                    else:
                        v.tensor_tensor(out=msl, in0=fstg,
                                        in1=scl[:, i // 2:i // 2 + 1].broadcast_to([128, 4096]), op=ALU.mult)
                    v.drain().then_inc(sV.h, 1)
                    sV.bump()

                cast_marks[i] = sV.n
                prep_hand["cast_last"] = sV.n
                mark_cast = sV.n
                ev_bank = prep_hand["ev"]
                for bi in range(4):
                    pt = pts_prep[bi]
                    ptb = pt[:].bitcast(BF16)
                    wbank = ev_bank[-4] if len(ev_bank) >= 4 else None

                    @block.tensor
                    def _(t, msl=msl, bi=bi, ptb=ptb, mcst=mark_cast, wb=wbank,
                          first=(first_w and i == 0 and bi == 0)):
                        t.wait_ge(sV.h, mcst)
                        if wb is not None:
                            t.wait_ge(sV.h, wb)
                        if first:
                            t.wait_ge(sA.h, mark_setup_act)
                        for kk in range(8):
                            t.transpose(ptb[:, kk * 128:(kk + 1) * 128],
                                        msl[:, (bi * 8 + kk) * 128:(bi * 8 + kk + 1) * 128], IDB[:])
                        t.matmul(p_s2[0:1, 0:1], ONE[0:1, 0:1], ONE[0:1, 0:1], start=True, stop=True)
                        t.drain().then_inc(sP.h, 1)
                        sP.bump()

                    mark_tp = sP.n

                    @block.vector
                    def _(v, asmv=asmv, bi=bi, ptb=ptb, mtp=mark_tp,
                          wa=(w_asm if bi == 0 else None)):
                        v.wait_ge(sP.h, mtp)
                        if wa is not None:
                            v.wait_ge(wa[0], wa[1])
                        v.tensor_tensor(out=asmv[:, bi * 1024:(bi + 1) * 1024], in0=ptb[:, 0:1024],
                                        in1=ONEB[:, 0:1].broadcast_to([128, 1024]), op=ALU.mult)
                        v.drain().then_inc(sV.h, 1)
                        sV.bump()

                    ev_bank.append(sV.n)

                tp_marks[i] = sP.n
                mark_ev = sV.n
                if pair:
                    @block.gpsimd
                    def _(g, i=i, asmv=asmv, me=mark_ev, scratch=scratch):
                        g.wait_ge(sV.h, me)
                        g.dma_start(out=scratch[2 * i:2 * i + 2].rearrange("m2 p kc mi -> p m2 kc mi"),
                                    in_=asmv.rearrange("p (m2 kc mi) -> p m2 kc mi", m2=2, mi=128)).then_inc(sN.h, 16)
                        sN.bump(16)
                else:
                    @block.gpsimd
                    def _(g, i=i, asmv=asmv, me=mark_ev, scratch=scratch):
                        ms, hh = i // 2, i % 2
                        g.wait_ge(sV.h, me)
                        g.dma_start(out=scratch[ms][:, hh * 32:(hh + 1) * 32, :],
                                    in_=asmv.rearrange("p (kc mi) -> p kc mi", mi=128)).then_inc(sN.h, 16)
                        sN.bump(16)
                st_marks[i] = (sN.h, sN.n)
                st_last[0] = (sN.h, sN.n)
            prep_hand["tp"] = sP.n
            prep_hand["st"] = st_last[0]

        prep_transposed(w_up, up_s, 64, 16, scl=UPS)
        prep_transposed(w_gate, gate_s, 64, 16, scl=GTS)
        prep_transposed(w_down, down_s, 16, 64, scl=DNS)

        if DBG2:
            @block.sync
            def _(s):
                s.wait_ge(sL.h, sL.n)
                s.wait_ge(sA.h, sA.n)
                for nm, scr in [("d_skall", skip_s), ("d_upall", up_s), ("d_gtall", gate_s), ("d_dnall", down_s)]:
                    s.dma_start(out=dbg[nm], in_=scr).then_inc(sS.h, 16)
                    sS.bump(16)

        prep_done = dict(L=sL.n, C=sC.n, P=sP.n, V=sV.n, A=sA.n, M=sM.n, N=sN.n)
        if DBG:
            @block.sync
            def _(s):
                s.wait_ge(sV.h, prep_done["V"])
                s.wait_ge(sA.h, prep_done["A"])
                s.wait_ge(sL.h, prep_done["L"])
                for nm, ap in [("d_h128", H128[:]), ("d_gam", GAM[:]), ("d_ups", UPS[:]),
                               ("d_su1", SU1[:]), ("d_bcb", BCB[:])]:
                    s.dma_start(out=dbg[nm], in_=ap).then_inc(sS.h, 16)
                    sS.bump(16)
                for nm, ap in [("d_up0", up_s[0]), ("d_dn0", down_s[0]), ("d_sk0", skip_s[0])]:
                    s.dma_start(out=dbg[nm], in_=ap).then_inc(sS.h, 16)
                    sS.bump(16)
            dbg_last[0] = sS.n

        store_mark = [0]
        down_done = [sP.n]

        def ln_stats(src3, scr3, scrf, waits, skip_square=False):
            @block.scalar
            def _(sc, waits=waits, skip_square=skip_square):
                for smh, val in waits:
                    sc.wait_ge(smh.h, val)
                if not skip_square:
                    sc.square(scrf[:, 0:8192], src3.rearrange("p c t -> p (c t)"))
                sc.drain().then_inc(sA.h, 1)
                sA.bump()

            m_sq = sA.n

            @block.vector
            def _(v, m=m_sq):
                v.wait_ge(sA.h, m)
                v.tensor_tensor(out=scr3[:, 0:8, :], in0=scr3[:, 0:8, :], in1=scr3[:, 8:16, :], op=ALU.add)
                v.tensor_tensor(out=scr3[:, 0:4, :], in0=scr3[:, 0:4, :], in1=scr3[:, 4:8, :], op=ALU.add)
                v.tensor_tensor(out=scr3[:, 0:2, :], in0=scr3[:, 0:2, :], in1=scr3[:, 2:4, :], op=ALU.add)
                v.tensor_tensor(out=scr3[:, 0:1, :], in0=scr3[:, 0:1, :], in1=scr3[:, 1:2, :], op=ALU.add)
                v.tensor_tensor(out=scr3[:, 8:16, :], in0=src3[:, 0:8, :], in1=src3[:, 8:16, :], op=ALU.add)
                v.tensor_tensor(out=scr3[:, 8:12, :], in0=scr3[:, 8:12, :], in1=scr3[:, 12:16, :], op=ALU.add)
                v.tensor_tensor(out=scr3[:, 8:10, :], in0=scr3[:, 8:10, :], in1=scr3[:, 10:12, :], op=ALU.add)
                v.tensor_tensor(out=scr3[:, 8:9, :], in0=scr3[:, 8:9, :], in1=scr3[:, 9:10, :], op=ALU.add)
                v.drain().then_inc(sV.h, 1)
                sV.bump()

            m_t1 = sV.n

            @block.tensor
            def _(t, m=m_t1):
                t.wait_ge(sV.h, m)
                t.matmul(p_s2[:], ONE[:], scr3[:, 0, :], start=True, stop=True)
                t.matmul(p_s1[:], ONE[:], scr3[:, 8, :], start=True, stop=True)
                t.drain().then_inc(sP.h, 1)
                sP.bump()

            mm_mark = sP.n

            @block.vector
            def _(v, m=mm_mark):
                v.wait_ge(sP.h, m)
                v.tensor_scalar(out=M_, in0=p_s1[:], scalar1=1.0 / D, scalar2=None, op0=ALU.mult)
                v.tensor_scalar(out=V_, in0=p_s2[:], scalar1=1.0 / D, scalar2=None, op0=ALU.mult)
                v.tensor_tensor(out=STG[:], in0=M_, in1=M_, op=ALU.mult)
                v.tensor_tensor(out=V_, in0=V_, in1=STG[:], op=ALU.subtract)
                v.drain().then_inc(sV.h, 1)
                sV.bump()

            m_var = sV.n

            @block.scalar
            def _(sc, m=m_var):
                sc.wait_ge(sV.h, m)
                sc.activation(out=R_, in_=V_, func=AF.Ln, bias=EPS, scale=1.0)
                sc.activation(out=R_, in_=R_, func=AF.Exp, bias=0.0, scale=-0.5)
                sc.drain().then_inc(sA.h, 1)
                sA.bump()

            return sA.n, mm_mark

        ln_prev_mm = [prep_done["P"]]

        for j in range(NT):
            t0 = j * P
            prev_store = store_mark[0]
            zfree = down_done[0]

            @block.sync
            def _(s, j=j, t0=t0, ps=prev_store):
                if ps:
                    s.wait_ge(sS.h, ps)
                if j == 0:
                    s.wait_ge(sL.h, prep_done["L"])
                    s.wait_ge(sA.h, prep_done["A"])
                    s.wait_ge(sP.h, prep_done["P"])
                    s.wait_ge(sV.h, prep_done["V"])
                    if prep_done["M"]:
                        s.wait_ge(sM.h, prep_done["M"])
                    if prep_done["N"]:
                        s.wait_ge(sN.h, prep_done["N"])
                s.dma_start(out=X[:], in_=xT3[:, :, t0:t0 + P]).then_inc(sL.h, 16)
                sL.bump(16)

            mark_x = sL.n

            @block.sync
            def _(s, zf=zfree):
                s.wait_ge(sP.h, zf)
                s.dma_start(out=WIN[:], in_=win_s).then_inc(sL.h, 16)
                sL.bump(16)
                s.dma_start(out=WOU, in_=wout_s).then_inc(sL.h, 16)
                sL.bump(16)

            mark_wz = sL.n

            skp_marks = {}
            for ms in range(2):
                @block.sync
                def _(s, ms=ms, zf=zfree):
                    s.wait_ge(sP.h, zf)
                    s.dma_start(out=SKP[:, ms, :].rearrange("p (kc mi) -> p kc mi", kc=16),
                                in_=skip_s[ms]).then_inc(sL.h, 16)
                    sL.bump(16)
                skp_marks[ms] = sL.n

            m_ln1, mm1 = ln_stats(X, B3, Bf, [(sL, mark_x), (sP, ln_prev_mm[0])])
            ln_prev_mm[0] = mm1

            if j == 0:
                dbg_tap("d_st1", STT[:], sA, m_ln1)
            @block.vector
            def _(v, m=m_ln1):
                v.wait_ge(sA.h, m)
                dbg_gate(v)
                for c in range(NCD):
                    v.tensor_tensor(out=STG[:], in0=X[:, c, :], in1=M_, op=ALU.subtract)
                    v.tensor_tensor(out=XB[:, c, :], in0=STG[:], in1=R_, op=ALU.mult)
                v.tensor_tensor(out=V_, in0=V_, in1=R_, op=ALU.mult)
                v.tensor_tensor(out=V_, in0=V_, in1=R_, op=ALU.mult)
                v.drain().then_inc(sV.h, 1)
                sV.bump()

            m_xin = sV.n
            if j == 0:
                dbg_tap("d_xin", XB[:], sV, m_xin)

            @block.scalar
            def _(sc, m=m_xin):
                sc.wait_ge(sV.h, m)
                sc.activation(out=R_, in_=V_, func=AF.Ln, bias=EPS, scale=1.0)
                sc.activation(out=R_, in_=R_, func=AF.Exp, bias=0.0, scale=-0.5)
                sc.drain().then_inc(sA.h, 1)
                sA.bump()

            m_r2 = sA.n

            @block.vector
            def _(v, m=m_r2, mw=mark_wz):
                v.wait_ge(sA.h, m)
                v.wait_ge(sL.h, mw)
                for c in range(NCD):
                    v.tensor_tensor(out=STG[:], in0=XB[:, c, :], in1=R_, op=ALU.mult)
                    v.scalar_tensor_tensor(out=UFL[:, c, :], in0=STG[:], scalar=GAM[:, c:c + 1],
                                           in1=BET[:, c:c + 1].broadcast_to([128, P]),
                                           op0=ALU.mult, op1=ALU.add)
                v.drain().then_inc(sV.h, 1)
                sV.bump()

            m_u = sV.n
            if j == 0:
                dbg_tap("d_u", UFL[:], sV, m_u)

            @block.tensor
            def _(t, m=m_u):
                t.wait_ge(sV.h, m)
                for mc, pt in ((0, p_a), (1, p_b), (2, p_c)):
                    for kc in range(NCD):
                        t.matmul(pt[:], WIN[:, kc, mc * 128:(mc + 1) * 128], UFL[:, kc, :],
                                 start=(kc == 0), stop=(kc == NCD - 1))
                t.drain().then_inc(sP.h, 1)
                sP.bump()

            m_gmm = sP.n

            @block.scalar
            def _(sc, m=m_gmm):
                sc.wait_ge(sP.h, m)
                sc.activation(out=AL[:, 0, :], in_=p_a[:], func=AF.Sigmoid, bias=NBIN[:, 0:1], scale=-1.0)
                sc.activation(out=STG[:], in_=p_a[:], func=AF.Sigmoid, bias=BIN[:, 0:1], scale=1.0)
                sc.activation(out=AL[:, 1, :], in_=p_b[:], func=AF.Identity, bias=BIN[:, 1:2], scale=1.0)
                sc.activation(out=AL[:, 2, :], in_=p_c[:], func=AF.Tanh, bias=BIN[:, 2:3], scale=1.0)
                sc.drain().then_inc(sA.h, 1)
                sA.bump()

            m_gev = sA.n

            @block.vector
            def _(v, m=m_gev):
                v.wait_ge(sA.h, m)
                v.tensor_tensor(out=AL[:, 1, :], in0=STG[:], in1=AL[:, 1, :], op=ALU.mult)
                v.tensor_tensor_scan(out=AL[:, 3, :], data0=AL[:, 0, :], data1=AL[:, 1, :],
                                     initial=MSC[:, 0:1], op0=ALU.mult, op1=ALU.add)
                v.tensor_tensor(out=YB, in0=AL[:, 3, :], in1=AL[:, 2, :], op=ALU.mult)
                v.drain().then_inc(sV.h, 1)
                sV.bump()

            m_y = sV.n

            @block.scalar
            def _(sc, m=m_y):
                sc.wait_ge(sV.h, m)
                sc.copy(MSC[:, 0:1], AL[:, 3, P - 1:P])
                sc.drain().then_inc(sA.h, 1)
                sA.bump()

            m_carry = sA.n
            if j == 0:
                dbg_tap("d_al", AL[:], sV, m_y)
            if DBG:
                @block.sync
                def _(s, j=j, m=m_y):
                    s.wait_ge(sV.h, m)
                    s.dma_start(out=dbg["d_car"][:, j:j+1], in_=MSC[:, 0:1], allow_slow_non_contiguous=True).then_inc(sS.h, 16)
                    sS.bump(16)
                    s.dma_start(out=dbg["d_h511"][:, j:j+1], in_=AL[:, 3, P-1:P], allow_slow_non_contiguous=True).then_inc(sS.h, 16)
                    sS.bump(16)
                dbg_last[0] = sS.n

            ao_ev = {}
            for mc in range(NCD):
                pt = p_a if mc % 2 == 0 else p_b

                @block.tensor
                def _(t, mc=mc, pt=pt, m=m_y, mw=mark_wz):
                    if mc == 0:
                        t.wait_ge(sV.h, m)
                        t.wait_ge(sL.h, mw)
                        dbg_gate(t)
                    t.wait_ge(sL.h, skp_marks[mc])
                    if mc >= 2:
                        t.wait_ge(sV.h, ao_ev[mc - 2])
                    t.matmul(pt[:], WOU[:, mc * 128:(mc + 1) * 128], YB, start=True, stop=False)
                    for kc in range(NCD):
                        t.matmul(pt[:], SKP[:, mc % 2, kc * 128:(kc + 1) * 128], XB[:, kc, :],
                                 start=False, stop=(kc == NCD - 1))
                    t.drain().then_inc(sP.h, 1)
                    sP.bump()

                m_ao = sP.n

                @block.vector
                def _(v, mc=mc, pt=pt, m=m_ao):
                    v.wait_ge(sP.h, m)
                    v.scalar_tensor_tensor(out=A3[:, mc, :], in0=pt[:], scalar=BCB[:, mc:mc + 1],
                                           in1=ONE[:, 0:1].broadcast_to([128, P]),
                                           op0=ALU.add, op1=ALU.mult)
                    v.drain().then_inc(sV.h, 1)
                    sV.bump()

                ao_ev[mc] = sV.n

                if mc + 2 < NCD:
                    m_cons = sP.n

                    @block.sync
                    def _(s, mc=mc, m=m_cons):
                        s.wait_ge(sP.h, m)
                        s.dma_start(out=SKP[:, mc % 2, :].rearrange("p (kc mi) -> p kc mi", kc=16),
                                    in_=skip_s[mc + 2]).then_inc(sL.h, 16)
                        sL.bump(16)
                    skp_marks[mc + 2] = sL.n

            m_ao_all = sV.n
            if j == 0:
                dbg_tap("d_ao", A3[:], sV, m_ao_all)

            m_ln3, mm3 = ln_stats(A3, B3, Bf, [(sV, m_ao_all), (sP, ln_prev_mm[0])])
            ln_prev_mm[0] = mm3

            @block.vector
            def _(v, m=m_ln3):
                v.wait_ge(sA.h, m)
                for c in range(NCD):
                    v.tensor_tensor(out=STG[:], in0=A3[:, c, :], in1=M_, op=ALU.subtract)
                    v.tensor_tensor(out=STG[:], in0=STG[:], in1=R_, op=ALU.mult)
                    v.tensor_tensor(out=X[:, c, :], in0=X[:, c, :], in1=STG[:], op=ALU.add)
                v.drain().then_inc(sV.h, 1)
                sV.bump()

            m_res1 = sV.n
            if j == 0:
                dbg_tap("d_x3", X[:], sV, m_res1)

            m_ln4, mm4 = ln_stats(X, B3, Bf, [(sV, m_res1), (sP, ln_prev_mm[0])])
            ln_prev_mm[0] = mm4

            @block.vector
            def _(v, m=m_ln4):
                v.wait_ge(sA.h, m)
                dbg_gate(v)
                for c in range(NCD):
                    v.tensor_tensor(out=STG[:], in0=X[:, c, :], in1=M_, op=ALU.subtract)
                    v.scalar_tensor_tensor(out=B3[:, c, :], in0=STG[:], scalar=SU1[:, c:c + 1],
                                           in1=R_, op0=ALU.mult, op1=ALU.mult)
                for st, (si, di) in enumerate([(B3, A3), (A3, B3), (B3, A3), (A3, XB)]):
                    h = 1 << st
                    ng = 16 // (2 * h)
                    siv = si.rearrange("p c t -> p (c t)").rearrange(
                        "p (g two i) -> p g two i", g=ng, two=2)
                    div = di.rearrange("p c t -> p (c t)").rearrange(
                        "p (g two i) -> p g two i", g=ng, two=2)
                    v.tensor_tensor(out=div[:, :, 0], in0=siv[:, :, 0], in1=siv[:, :, 1], op=ALU.add)
                    v.tensor_tensor(out=div[:, :, 1], in0=siv[:, :, 0], in1=siv[:, :, 1], op=ALU.subtract)
                v.drain().then_inc(sV.h, 1)
                sV.bump()

            m_x1 = sV.n

            f1_ev = {}
            for cb in range(4):
                @block.tensor
                def _(t, cb=cb, m=m_x1):
                    if cb == 0:
                        t.wait_ge(sV.h, m)
                    if cb >= 1:
                        t.wait_ge(sA.h, f1_ev[cb - 1])
                    for i in range(4):
                        t.matmul(pts4[i][:], H128[:], XB[:, cb * 4 + i, :], start=True, stop=True)
                    t.drain().then_inc(sP.h, 1)
                    sP.bump()

                m_f1 = sP.n

                @block.scalar
                def _(sc, cb=cb, m=m_f1):
                    sc.wait_ge(sP.h, m)
                    for i in range(4):
                        sc.copy(XB[:, cb * 4 + i, :], pts4[i][:])
                    sc.drain().then_inc(sA.h, 1)
                    sA.bump()

                f1_ev[cb] = sA.n

            m_x1h = sA.n
            if j == 0:
                dbg_tap("d_x1", XB[:], sA, m_x1h)

            up_marks = {}
            for m0 in range(2):
                @block.sync
                def _(s, m0=m0, zf=zfree, j=j):
                    s.wait_ge(sP.h, zf)
                    if j == 0 and prep_done["N"]:
                        s.wait_ge(sN.h, prep_done["N"])
                    s.dma_start(out=WSu[m0].rearrange("p (kc mi) -> p kc mi", kc=16),
                                in_=up_s[m0]).then_inc(sL.h, 16)
                    sL.bump(16)
                    s.dma_start(out=WSg[m0].rearrange("p (kc mi) -> p kc mi", kc=16),
                                in_=gate_s[m0]).then_inc(sL.h, 16)
                    sL.bump(16)
                up_marks[m0] = [(sL.h, sL.n)]

            evu_marks = {}
            z_marks = {}
            for m in range(NMI):
                ubuf, gbuf = WSu[m % 2], WSg[m % 2]
                ptu = p_a if m % 2 == 0 else p_b
                ptg = p_c if m % 2 == 0 else p_d

                @block.tensor
                def _(t, m=m, ubuf=ubuf, gbuf=gbuf, ptu=ptu, ptg=ptg, mh=m_x1h):
                    if m == 0:
                        t.wait_ge(sA.h, mh)
                    for _smh, _sval in up_marks[m]:
                        t.wait_ge(_smh, _sval)
                    if m >= 2:
                        t.wait_ge(sA.h, evu_marks[m - 2])
                        t.wait_ge(sV.h, z_marks[m - 2])
                    for kc in range(NCD):
                        t.matmul(ptu[:], ubuf[:, kc * 128:(kc + 1) * 128], XB[:, kc, :],
                                 start=(kc == 0), stop=(kc == NCD - 1))
                    for kc in range(NCD):
                        t.matmul(ptg[:], gbuf[:, kc * 128:(kc + 1) * 128], XB[:, kc, :],
                                 start=(kc == 0), stop=(kc == NCD - 1))
                    if m >= 2:
                        pH = p_s1 if m % 2 == 0 else p_s2
                        t.matmul(pH[:], H128[:], Z3[:, m - 2, :], start=True, stop=True)
                    t.drain().then_inc(sP.h, 1)
                    sP.bump()

                m_ug = sP.n

                @block.scalar
                def _(sc, m=m, ptg=ptg, mug=m_ug):
                    sc.wait_ge(sP.h, mug)
                    if m >= 2:
                        sc.wait_ge(sV.h, z_marks[m - 2])
                    o = (m % 2) * 1024
                    sc.activation(out=Bf[:, o + 512:o + 1024], in_=ptg[:], func=AF.Silu,
                                  bias=0.0, scale=1.0)
                    if m + 2 < NMI:
                        sc.dma_start(out=WSu[m % 2].rearrange("p (kc mi) -> p kc mi", kc=16),
                                     in_=up_s[m + 2]).then_inc(sM.h, 16)
                        sM.bump(16)
                        sc.dma_start(out=WSg[m % 2].rearrange("p (kc mi) -> p kc mi", kc=16),
                                     in_=gate_s[m + 2]).then_inc(sM.h, 16)
                        sM.bump(16)
                    sc.drain().then_inc(sA.h, 1)
                    sA.bump()

                if m + 2 < NMI:
                    up_marks[m + 2] = [(sM.h, sM.n)]
                evu_marks[m] = sA.n
                if DBG and j == 0 and m == 40:
                    dbg_tap("d_upv", Bf[:, 0:1024], sA, evu_marks[m])
                    dbg_tap("d_ups2", UPS[:], sA, evu_marks[m])

                @block.vector
                def _(v, m=m, ptu=ptu, me=evu_marks[m]):
                    v.wait_ge(sA.h, me)
                    if DBG and j == 0 and m in (40, 41, 42):
                        dbg_gate(v)
                    o = (m % 2) * 1024
                    v.tensor_tensor(out=Z3[:, m, :], in0=ptu[:],
                                    in1=Bf[:, o + 512:o + 1024], op=ALU.mult)
                    if m >= 2:
                        pH = p_s1 if m % 2 == 0 else p_s2
                        v.tensor_tensor(out=Z3[:, m - 2, :], in0=pH[:],
                                        in1=ONE[:, 0:1].broadcast_to([128, P]), op=ALU.mult)
                    v.drain().then_inc(sV.h, 1)
                    sV.bump()

                z_marks[m] = sV.n
                if DBG and j == 0:
                    @block.gpsimd
                    def _(g, m=m, zm=z_marks[m]):
                        g.wait_ge(sV.h, zm)
                        g.dma_start(out=dbg["d_z2"][:, m, :], in_=Z3[:, m, :]).then_inc(sS.h, 16)
                        sS.bump(16)
                    dbg_last[0] = sS.n

            @block.tensor
            def _(t, zl=z_marks[NMI - 1]):
                t.wait_ge(sV.h, zl)
                t.matmul(p_s1[:], H128[:], Z3[:, NMI - 2, :], start=True, stop=True)
                t.matmul(p_s2[:], H128[:], Z3[:, NMI - 1, :], start=True, stop=True)
                t.drain().then_inc(sP.h, 1)
                sP.bump()

            m_hz = sP.n

            @block.vector
            def _(v, m=m_hz):
                v.wait_ge(sP.h, m)
                v.tensor_tensor(out=Z3[:, NMI - 2, :], in0=p_s1[:],
                                in1=ONE[:, 0:1].broadcast_to([128, P]), op=ALU.mult)
                v.tensor_tensor(out=Z3[:, NMI - 1, :], in0=p_s2[:],
                                in1=ONE[:, 0:1].broadcast_to([128, P]), op=ALU.mult)
                v.drain().then_inc(sV.h, 1)
                sV.bump()

            m_z_all = sV.n
            m_ug_all = sP.n

            T1b = A[:, 0:2048].bitcast(BF16).rearrange("p (c t) -> p c t", c=64)
            T2b = A[:, 2048:4096].bitcast(BF16).rearrange("p (c t) -> p c t", c=64)

            @block.vector
            def _(v, m=m_z_all, T1=T1b, T2=T2b):
                v.wait_ge(sV.h, m)
                dbg_gate(v)
                for g in range(P // 64):
                    zg = Z3[:, :, g * 64:(g + 1) * 64]
                    bufs = [zg, T1, T2, T1, T2, T1, zg]
                    for st in range(6):
                        h = 1 << st
                        ng = 64 // (2 * h)
                        si = bufs[st].rearrange("p (g2 two i) t -> p g2 two i t", g2=ng, two=2)
                        di = bufs[st + 1].rearrange("p (g2 two i) t -> p g2 two i t", g2=ng, two=2)
                        v.tensor_tensor(out=di[:, :, 0], in0=si[:, :, 0], in1=si[:, :, 1], op=ALU.add)
                        v.tensor_tensor(out=di[:, :, 1], in0=si[:, :, 0], in1=si[:, :, 1], op=ALU.subtract)
                v.drain().then_inc(sV.h, 1)
                sV.bump()

            m_h64 = sV.n
            if j == 0:
                dbg_tap("d_v", Z3[:], sV, m_h64)

            dn_marks = {}
            for hh in range(2):
                @block.sync
                def _(s, hh=hh, m=m_ug_all, j=j):
                    s.wait_ge(sP.h, m)
                    if j == 0 and prep_done["N"]:
                        s.wait_ge(sN.h, prep_done["N"])
                    s.dma_start(out=WSd[hh].rearrange("p (kc mi) -> p kc mi", kc=32),
                                in_=down_s[0, :, hh * 32:(hh + 1) * 32, :]).then_inc(sL.h, 16)
                    sL.bump(16)
                dn_marks[(0, hh)] = sL.n

            dn_ev = {}
            for mc in range(NCD):
                pt = p_a if mc % 2 == 0 else p_b
                for hh in range(2):
                    @block.tensor
                    def _(t, mc=mc, hh=hh, pt=pt, m=m_h64):
                        if mc == 0 and hh == 0:
                            t.wait_ge(sV.h, m)
                        t.wait_ge(sL.h, dn_marks[(mc, hh)])
                        if mc >= 2 and hh == 0:
                            t.wait_ge(sA.h, dn_ev[mc - 2])
                        for i in range(32):
                            kc = hh * 32 + i
                            t.matmul(pt[:], WSd[hh][:, i * 128:(i + 1) * 128], Z3[:, kc, :],
                                     start=(kc == 0), stop=(kc == 63))
                        t.drain().then_inc(sP.h, 1)
                        sP.bump()

                    m_dn = sP.n
                    nxt = (mc + 1, hh)
                    if nxt[0] < NCD and nxt not in dn_marks:
                        @block.sync
                        def _(s, hh=hh, nxt=nxt, m=m_dn):
                            s.wait_ge(sP.h, m)
                            s.dma_start(out=WSd[hh].rearrange("p (kc mi) -> p kc mi", kc=32),
                                        in_=down_s[nxt[0], :, nxt[1] * 32:(nxt[1] + 1) * 32, :]).then_inc(sL.h, 16)
                            sL.bump(16)
                        dn_marks[nxt] = sL.n

                m_dmm = sP.n

                @block.scalar
                def _(sc, mc=mc, pt=pt, m=m_dmm):
                    sc.wait_ge(sP.h, m)
                    sc.copy(B3[:, mc, :], pt[:])
                    sc.drain().then_inc(sA.h, 1)
                    sA.bump()

                dn_ev[mc] = sA.n

            down_done[0] = sP.n
            m_fo_all = sA.n
            if j == 0:
                dbg_tap("d_fo", B3[:], sA, m_fo_all)

            m_ln5, mm5 = ln_stats(B3, A3, Af, [(sA, m_fo_all), (sP, ln_prev_mm[0])])
            ln_prev_mm[0] = mm5

            @block.vector
            def _(v, m=m_ln5):
                v.wait_ge(sA.h, m)
                for c in range(NCD):
                    v.tensor_tensor(out=STG[:], in0=B3[:, c, :], in1=M_, op=ALU.subtract)
                    v.tensor_tensor(out=STG[:], in0=STG[:], in1=R_, op=ALU.mult)
                    v.tensor_tensor(out=X[:, c, :], in0=X[:, c, :], in1=STG[:], op=ALU.add)
                v.drain().then_inc(sV.h, 1)
                sV.bump()

            m_out = sV.n

            @block.scalar
            def _(sc, t0=t0, m=m_out):
                sc.wait_ge(sV.h, m)
                sc.dma_start(out=oT3[:, :, t0:t0 + P], in_=X[:]).then_inc(sS.h, 16)
                sS.bump(16)

            store_mark[0] = sS.n

        @block.sync
        def _(s):
            s.wait_ge(sS.h, sS.n)

    return nc


_NC_CACHE = [None]


def kernel(**inputs):
    x = np.asarray(inputs["L0_attn_x"])
    in_common = {
        "w_in": np.asarray(inputs["L0_attn_W_in"]),
        "w_out": np.asarray(inputs["L0_attn_W_out"]),
        "w_skip": np.asarray(inputs["L0_attn_W_skip"]),
        "w_up": np.asarray(inputs["L0_ffn_W_up"]),
        "w_gate": np.asarray(inputs["L0_ffn_W_gate"]),
        "w_down": np.asarray(inputs["L0_ffn_W_down"]),
        "v_gamma": np.asarray(inputs["L0_attn_gamma"]),
        "v_beta": np.asarray(inputs["L0_attn_beta"]),
        "v_bin": np.asarray(inputs["L0_attn_b_in"]),
        "v_bout": np.asarray(inputs["L0_attn_b_out"]),
        "v_bskip": np.asarray(inputs["L0_attn_b_skip"]),
        "v_suug": np.asarray(inputs["L0_ffn_SU_ug"]),
        "v_sudp": np.asarray(inputs["L0_ffn_SU_dp"]),
        "v_wsug": np.asarray(inputs["L0_ffn_Wscale_ug"]),
        "v_wsdp": np.asarray(inputs["L0_ffn_Wscale_dp"]),
    }
    if _NC_CACHE[0] is None:
        _NC_CACHE[0] = build_nc()
    nc = _NC_CACHE[0]
    in_maps = []
    for b in range(B):
        m = dict(in_common)
        m["xT"] = np.ascontiguousarray(x[b].T)
        in_maps.append(m)
    res = run_bass_kernel_spmd(nc, in_maps, core_ids=list(range(B)))
    out = np.empty((B, T, D), dtype=np.float32)
    for b in range(B):
        out[b] = np.asarray(res.results[b]["oT"], dtype=np.float32).T
    return out


# version 17
# speedup vs baseline: 1.0687x; 1.0687x over previous
import contextlib
import math
import numpy as np
import concourse.bass as bass
import concourse.mybir as mybir
from concourse.bass_utils import run_bass_kernel_spmd

F32 = mybir.dt.float32
BF16 = mybir.dt.bfloat16
AF = mybir.ActivationFunctionType
ALU = mybir.AluOpType

B, T, D, SDIM = 8, 4096, 2048, 128
INTER = 8192
P = 512
NT = T // P
NCD = D // 128
NMI = INTER // 128
EPS = 1e-5
C_SU1 = 1.0 / (64.0 * math.sqrt(2048.0))
C_UP = 64.0
C_DN = 1.0 / math.sqrt(8192.0)
DBG = False
DBG2 = False


class Sem:
    def __init__(self, h):
        self.h = h
        self.n = 0

    def bump(self, k=1):
        self.n += k
        return self.n


def build_nc():
    nc = bass.Bass("TRN2", target_bir_lowering=False)

    xT = nc.dram_tensor("xT", [D, T], F32, kind="ExternalInput").ap()
    w_in = nc.dram_tensor("w_in", [D, 3 * SDIM], F32, kind="ExternalInput").ap()
    w_out = nc.dram_tensor("w_out", [SDIM, D], F32, kind="ExternalInput").ap()
    w_skip = nc.dram_tensor("w_skip", [D, D], F32, kind="ExternalInput").ap()
    w_up = nc.dram_tensor("w_up", [INTER, D], F32, kind="ExternalInput").ap()
    w_gate = nc.dram_tensor("w_gate", [INTER, D], F32, kind="ExternalInput").ap()
    w_down = nc.dram_tensor("w_down", [D, INTER], F32, kind="ExternalInput").ap()
    v_gamma = nc.dram_tensor("v_gamma", [D], F32, kind="ExternalInput").ap()
    v_beta = nc.dram_tensor("v_beta", [D], F32, kind="ExternalInput").ap()
    v_bin = nc.dram_tensor("v_bin", [3 * SDIM], F32, kind="ExternalInput").ap()
    v_bout = nc.dram_tensor("v_bout", [D], F32, kind="ExternalInput").ap()
    v_bskip = nc.dram_tensor("v_bskip", [D], F32, kind="ExternalInput").ap()
    v_suug = nc.dram_tensor("v_suug", [D], F32, kind="ExternalInput").ap()
    v_sudp = nc.dram_tensor("v_sudp", [INTER], F32, kind="ExternalInput").ap()
    v_wsug = nc.dram_tensor("v_wsug", [2 * INTER], F32, kind="ExternalInput").ap()
    v_wsdp = nc.dram_tensor("v_wsdp", [D], F32, kind="ExternalInput").ap()
    oT = nc.dram_tensor("oT", [D, T], F32, kind="ExternalOutput").ap()

    dbg = {}
    if DBG2:
        for nm, shape in [("d_skall", [16, 128, 16, 128]), ("d_upall", [64, 128, 16, 128]),
                          ("d_gtall", [64, 128, 16, 128]), ("d_dnall", [16, 128, 64, 128])]:
            dbg[nm] = nc.dram_tensor(nm, shape, BF16, kind="ExternalOutput").ap()
    if DBG:
        for nm, shape, dt in [
            ("d_xin", [128, 16, 512], BF16), ("d_u", [128, 16, 512], BF16),
            ("d_al", [128, 4, 512], F32), ("d_ao", [128, 16, 512], F32),
            ("d_x3", [128, 16, 512], F32), ("d_x1", [128, 16, 512], BF16),
            ("d_z", [128, 64, 512], BF16), ("d_v", [128, 64, 512], BF16),
            ("d_fo", [128, 16, 512], F32), ("d_st1", [128, 3, 512], F32),
            ("d_h128", [128, 128], BF16), ("d_gam", [128, 16], F32),
            ("d_ups", [128, 64], F32), ("d_su1", [128, 16], F32),
            ("d_bcb", [128, 16], F32), ("d_up0", [128, 16, 128], BF16),
            ("d_dn0", [128, 64, 128], BF16), ("d_sk0", [128, 16, 128], BF16), ("d_ups2", [128, 64], F32), ("d_upv", [128, 1024], F32), ("d_z2", [128, 64, 512], BF16), ("d_car", [128, 8], F32), ("d_h511", [128, 8], F32),
        ]:
            dbg[nm] = nc.dram_tensor(nm, shape, dt, kind="ExternalOutput").ap()
    win_s = nc.dram_tensor("win_s", [128, 16, 384], BF16, kind="Internal").ap()
    wout_s = nc.dram_tensor("wout_s", [128, 2048], BF16, kind="Internal").ap()
    skip_s = nc.dram_tensor("skip_s", [16, 128, 16, 128], BF16, kind="Internal").ap()
    up_s = nc.dram_tensor("up_s", [64, 128, 16, 128], BF16, kind="Internal").ap()
    gate_s = nc.dram_tensor("gate_s", [64, 128, 16, 128], BF16, kind="Internal").ap()
    down_s = nc.dram_tensor("down_s", [16, 128, 64, 128], BF16, kind="Internal").ap()

    with contextlib.ExitStack() as es:
        E = es.enter_context
        X = E(nc.sbuf_tensor("X", [128, NCD, P], F32))
        A = E(nc.sbuf_tensor("A", [128, 8192], F32))
        Bf = E(nc.sbuf_tensor("Bf", [128, 8192], F32))
        STT = E(nc.sbuf_tensor("STT", [128, 3, 512], F32))
        STG = E(nc.sbuf_tensor("STG", [128, 512], F32))
        MSC = E(nc.sbuf_tensor("MSC", [128, 16], F32))
        ONE = E(nc.sbuf_tensor("ONE", [128, 128], F32))
        GAM = E(nc.sbuf_tensor("GAM", [128, 16], F32))
        BET = E(nc.sbuf_tensor("BET", [128, 16], F32))
        BIN = E(nc.sbuf_tensor("BIN", [128, 3], F32))
        NBIN = E(nc.sbuf_tensor("NBIN", [128, 3], F32))
        BCB = E(nc.sbuf_tensor("BCB", [128, 16], F32))
        SU1 = E(nc.sbuf_tensor("SU1", [128, 16], F32))
        UPS = E(nc.sbuf_tensor("UPS", [128, 64], F32))
        GTS = E(nc.sbuf_tensor("GTS", [128, 64], F32))
        DNS = E(nc.sbuf_tensor("DNS", [128, 16], F32))
        T16 = E(nc.sbuf_tensor("T16", [128, 16], F32))
        T64 = E(nc.sbuf_tensor("T64", [128, 64], F32))
        Z = E(nc.sbuf_tensor("Z", [128, 32768], BF16))
        XB = E(nc.sbuf_tensor("XB", [128, NCD, P], BF16))
        WS = E(nc.sbuf_tensor("WS", [128, 8192], BF16))
        H128 = E(nc.sbuf_tensor("H128", [128, 128], BF16))
        IDB = E(nc.sbuf_tensor("IDB", [128, 128], BF16))
        ONEB = E(nc.sbuf_tensor("ONEB", [128, 1], BF16))
        p_a = E(nc.psum_tensor("p_a", [128, 512], F32))
        p_b = E(nc.psum_tensor("p_b", [128, 512], F32))
        p_c = E(nc.psum_tensor("p_c", [128, 512], F32))
        p_d = E(nc.psum_tensor("p_d", [128, 512], F32))
        p_s1 = E(nc.psum_tensor("p_s1", [128, 512], F32))
        p_s2 = E(nc.psum_tensor("p_s2", [128, 512], F32))
        sL = Sem(E(nc.semaphore("sL")))
        sC = Sem(E(nc.semaphore("sC")))
        sP = Sem(E(nc.semaphore("sP")))
        sV = Sem(E(nc.semaphore("sV")))
        sA = Sem(E(nc.semaphore("sA")))
        sS = Sem(E(nc.semaphore("sS")))
        sM = Sem(E(nc.semaphore("sM")))
        sN = Sem(E(nc.semaphore("sN")))
        block = E(nc.Block())

        nc.const_aps.register(F32, EPS, MSC[:, 1:2])

        pts4 = (p_a, p_b, p_c, p_d)
        Af = A[:]
        A3 = A.rearrange("p (c t) -> p c t", c=16)
        B3 = Bf.rearrange("p (c t) -> p c t", c=16)
        T1 = A[:, 0:4096].rearrange("p (c t) -> p c t", c=64)
        T2 = A[:, 4096:8192].rearrange("p (c t) -> p c t", c=64)
        AL = A[:, 0:2048].rearrange("p (s t) -> p s t", s=4)
        Z3 = Z.rearrange("p (c t) -> p c t", c=64)
        WIN = Z[:, 0:6144].rearrange("p (c m) -> p c m", c=16)
        WOU = Z[:, 6144:8192]
        SKP = Z[:, 8192:12288].rearrange("p (b q) -> p b q", b=2)
        UFL = Z[:, 12288:20480].rearrange("p (c t) -> p c t", c=16)
        YB = Z[:, 20480:20992]
        MSLAB = Z
        ASM = XB.rearrange("p c t -> p (c t)")
        Xfl = X.rearrange("p c t -> p (c t)")
        ROW = Xfl[0:1, :]
        IO2 = Bf[:, 0:128]
        IDF = Bf[:, 128:256]
        WSu = (WS[:, 0:2048], WS[:, 2048:4096])
        WSg = (WS[:, 4096:6144], WS[:, 6144:8192])
        WSd = (WS[:, 0:4096], WS[:, 4096:8192])
        M_ = STT[:, 0, :]
        R_ = STT[:, 1, :]
        V_ = STT[:, 2, :]
        xT3 = xT.rearrange("(c p) t -> p c t", p=128)
        oT3 = oT.rearrange("(c p) t -> p c t", p=128)
        dbg_last = [0]

        def dbg_tap(name, src_ap, wsem=None, wval=None):
            if not DBG:
                return
            @block.sync
            def _(s, name=name, src_ap=src_ap, wsem=wsem, wval=wval):
                if wsem is not None:
                    s.wait_ge(wsem.h, wval)
                s.dma_start(out=dbg[name], in_=src_ap).then_inc(sS.h, 16)
                sS.bump(16)
            dbg_last[0] = sS.n

        def dbg_gate(eng):
            if DBG:
                eng.wait_ge(sS.h, dbg_last[0])

        @block.gpsimd
        def _(g):
            g.memset(MSC[:, 0:1], 0.0)
            g.memset(MSC[:, 1:2], EPS)
            g.memset(ONE[:], 1.0)
            g.iota(MSC[:, 2:3], pattern=[[0, 1]], base=0, channel_multiplier=1,
                   allow_small_or_imprecise_dtypes=True)
            g.iota(IO2, pattern=[[1, 128]], base=0, channel_multiplier=0,
                   allow_small_or_imprecise_dtypes=True)
            g.tensor_scalar(out=IDF, in0=IO2, scalar1=MSC[:, 2:3], scalar2=None, op0=ALU.is_equal)
            g.drain().then_inc(sC.h, 1)
            sC.bump(1)

        mark_iota = sC.n

        @block.scalar
        def _(sc):
            sc.wait_ge(sC.h, mark_iota)
            sc.copy(IDB[:], IDF)
            sc.copy(ONEB[:], ONE[:, 0:1])
            sc.drain().then_inc(sA.h, 1)
            sA.bump()

        mark_idb = sA.n

        @block.vector
        def _(v):
            v.wait_ge(sC.h, mark_iota)
            v.wait_ge(sA.h, mark_idb)
            src, dst = IDF, STG[:, 0:128]
            for h in (1, 2, 4, 8, 16, 32, 64):
                ng = 128 // (2 * h)
                si = src.rearrange("p (g two i) -> p g two i", g=ng, two=2)
                di = dst.rearrange("p (g two i) -> p g two i", g=ng, two=2)
                v.tensor_tensor(out=di[:, :, 0], in0=si[:, :, 0], in1=si[:, :, 1], op=ALU.add)
                v.tensor_tensor(out=di[:, :, 1], in0=si[:, :, 0], in1=si[:, :, 1], op=ALU.subtract)
                src, dst = dst, src
            v.drain().then_inc(sV.h, 1)
            sV.bump()

        mark_h128_dve = sV.n
        h128_src = STG[:, 0:128]

        @block.scalar
        def _(sc):
            sc.wait_ge(sV.h, mark_h128_dve)
            sc.copy(H128[:], h128_src)
            sc.drain().then_inc(sA.h, 1)
            sA.bump()

        mark_setup_act = sA.n
        mark_setup = sV.n

        waves = [
            [(v_gamma, D, GAM), (v_beta, D, BET), (v_bin, 3 * SDIM, BIN), (v_bout, D, BCB)],
            [(v_bskip, D, T16), (v_suug, D, SU1), (v_wsdp, D, DNS)],
            [(v_sudp, INTER, T64)],
            [(v_wsug[0:INTER], INTER, UPS)],
            [(v_wsug[INTER:2 * INTER], INTER, GTS)],
        ]
        for wv in waves:
            offs = []
            o = 0
            for dv, ln, dst in wv:
                offs.append(o)
                o += ln
            prev_evac = sA.n

            @block.sync
            def _(s, wv=wv, offs=offs, pe=prev_evac):
                s.wait_ge(sA.h, pe)
                for (dv, ln, dst), off in zip(wv, offs):
                    s.dma_start(out=ROW[:, off:off + ln], in_=dv.unsqueeze(0)).then_inc(sL.h, 16)
                    sL.bump(16)

            mark_row = sL.n

            @block.tensor
            def _(t, wv=wv, offs=offs, mr=mark_row):
                t.wait_ge(sL.h, mr)
                t.wait_ge(sC.h, mark_iota)
                col = 0
                for (dv, ln, dst), off in zip(wv, offs):
                    for i in range(ln // 128):
                        t.matmul(p_s1[:, col:col + 1], ROW[:, off + i * 128:off + (i + 1) * 128],
                                 ONE[0:1, 0:1], start=True, stop=True)
                        col += 1
                t.drain().then_inc(sP.h, 1)
                sP.bump()

            mark_mm = sP.n

            @block.scalar
            def _(sc, wv=wv, mm=mark_mm):
                sc.wait_ge(sP.h, mm)
                col = 0
                for dv, ln, dst in wv:
                    nc_ = ln // 128
                    sc.copy(dst[:, 0:nc_], p_s1[:, col:col + nc_])
                    col += nc_
                sc.drain().then_inc(sA.h, 1)
                sA.bump()

        mark_vecs = sA.n

        @block.vector
        def _(v):
            v.wait_ge(sA.h, mark_vecs)
            v.wait_ge(sC.h, mark_iota)
            v.tensor_tensor(out=BCB[:], in0=BCB[:], in1=T16[:], op=ALU.add)
            v.scalar_tensor_tensor(out=SU1[:], in0=SU1[:], scalar=C_SU1, in1=ONE[:, 0:16],
                                   op0=ALU.mult, op1=ALU.mult)
            v.scalar_tensor_tensor(out=UPS[:], in0=UPS[:], scalar=C_UP, in1=T64[:],
                                   op0=ALU.mult, op1=ALU.mult)
            v.scalar_tensor_tensor(out=GTS[:], in0=GTS[:], scalar=C_UP, in1=ONE[:, 0:64],
                                   op0=ALU.mult, op1=ALU.mult)
            v.scalar_tensor_tensor(out=DNS[:], in0=DNS[:], scalar=C_DN, in1=ONE[:, 0:16],
                                   op0=ALU.mult, op1=ALU.mult)
            v.scalar_tensor_tensor(out=NBIN[:], in0=BIN[:], scalar=-1.0, in1=ONE[:, 0:3],
                                   op0=ALU.mult, op1=ALU.mult)
            v.drain().then_inc(sV.h, 1)
            sV.bump()

        @block.gpsimd
        def _(g):
            g.dma_start(out=WIN[:], in_=w_in.rearrange("(kc p) m -> p kc m", p=128)).then_inc(sC.h, 16)
            sC.bump(16)
            g.dma_start(out=WOU, in_=w_out).then_inc(sC.h, 16)
            sC.bump(16)

        mark_wz_prep = sC.n

        @block.sync
        def _(s):
            s.wait_ge(sC.h, mark_wz_prep)
            s.dma_start(out=win_s, in_=WIN[:]).then_inc(sL.h, 16)
            sL.bump(16)
            s.dma_start(out=wout_s, in_=WOU).then_inc(sL.h, 16)
            sL.bump(16)

        mark_wz_store = sL.n

        w_skip4 = w_skip.rearrange("(kc p) (ms mi) -> ms p kc mi", p=128, mi=128)
        skip_store = {}
        for ms in range(16):
            half = ms % 2
            asmv = ASM[:, half * 2048:(half + 1) * 2048]
            wprev = skip_store.get(ms - 2)

            @block.gpsimd
            def _(g, ms=ms, asmv=asmv, wprev=wprev):
                if wprev is not None:
                    g.wait_ge(sL.h, wprev)
                g.dma_start(out=asmv.rearrange("p (kc mi) -> p kc mi", kc=16),
                            in_=w_skip4[ms]).then_inc(sC.h, 16)
                sC.bump(16)

            mark_cast = sC.n

            @block.sync
            def _(s, ms=ms, asmv=asmv, mcst=mark_cast):
                s.wait_ge(sC.h, mcst)
                s.dma_start(out=skip_s[ms], in_=asmv.rearrange("p (kc mi) -> p kc mi", kc=16)).then_inc(sL.h, 16)
                sL.bump(16)

            skip_store[ms] = sL.n

        prep_hand = {"tp": None, "st": (sL.h, sL.n), "ev": [], "mslL": mark_wz_store,
                     "first_weight": True, "ld": []}

        def prep_transposed(wsrc, scratch, n_ms, n_kc, scl=None):
            w3 = wsrc.rearrange("(ms p) k -> ms p k", p=128)
            kcols = n_kc * 128
            U = 4096
            pair = kcols == 2048
            n_units = (n_ms * kcols) // U
            cast_marks = {}
            tp_marks = {}
            st_marks = {}
            st_last = [prep_hand["st"]]
            first_w = prep_hand["first_weight"]
            prep_hand["first_weight"] = False
            pts_prep = (p_c, p_d, p_a, p_b)
            for i in range(n_units):
                par = i % 2
                fstg = Bf[:, par * U:(par + 1) * U]
                msl = MSLAB[:, par * U:(par + 1) * U]
                if pair:
                    asmv = ASM[:, par * U:(par + 1) * U]
                    srcap = w3[2 * i:2 * i + 2].rearrange("m2 p k -> p m2 k")
                else:
                    ms, hh = i // 2, i % 2
                    asmv = ASM[:, hh * U:(hh + 1) * U]
                    srcap = w3[ms][:, hh * U:(hh + 1) * U]
                w_fstg = cast_marks.get(i - 2, prep_hand.get("cast_last"))
                w_msl = tp_marks.get(i - 2, prep_hand["tp"])
                w_asm = st_marks.get(i - 2, prep_hand["st"])

                if i % 2 == 0:
                    @block.sync
                    def _(s, srcap=srcap, fstg=fstg, wf=w_fstg, gate_idf=(first_w and i < 2)):
                        if wf is not None:
                            s.wait_ge(sV.h, wf)
                        if gate_idf:
                            s.wait_ge(sA.h, mark_setup_act)
                        s.dma_start(out=fstg, in_=srcap).then_inc(sL.h, 16)
                        sL.bump(16)

                    mark_ld = (sL.h, sL.n)
                else:
                    @block.scalar
                    def _(sc, srcap=srcap, fstg=fstg, wf=w_fstg, gate_idf=(first_w and i < 2)):
                        if wf is not None:
                            sc.wait_ge(sV.h, wf)
                        if gate_idf:
                            sc.wait_ge(sA.h, mark_setup_act)
                        sc.dma_start(out=fstg, in_=srcap).then_inc(sM.h, 16)
                        sM.bump(16)

                    mark_ld = (sM.h, sM.n)

                @block.vector
                def _(v, i=i, fstg=fstg, msl=msl, ml=mark_ld, wm=w_msl, first2=(first_w and i < 2)):
                    v.wait_ge(ml[0], ml[1])
                    if wm is not None:
                        v.wait_ge(sP.h, wm)
                    if first2 and prep_hand["mslL"]:
                        v.wait_ge(sL.h, prep_hand["mslL"])
                    if first2:
                        v.wait_ge(sC.h, mark_iota)
                    if scl is None:
                        v.tensor_tensor(out=msl, in0=fstg,
                                        in1=ONE[:, 0:1].broadcast_to([128, 4096]), op=ALU.mult)
                    elif pair:
                        v.tensor_tensor(out=msl[:, 0:2048], in0=fstg[:, 0:2048],
                                        in1=scl[:, 2 * i:2 * i + 1].broadcast_to([128, 2048]), op=ALU.mult)
                        v.tensor_tensor(out=msl[:, 2048:4096], in0=fstg[:, 2048:4096],
                                        in1=scl[:, 2 * i + 1:2 * i + 2].broadcast_to([128, 2048]), op=ALU.mult)
                    else:
                        v.tensor_tensor(out=msl, in0=fstg,
                                        in1=scl[:, i // 2:i // 2 + 1].broadcast_to([128, 4096]), op=ALU.mult)
                    v.drain().then_inc(sV.h, 1)
                    sV.bump()

                cast_marks[i] = sV.n
                prep_hand["cast_last"] = sV.n
                mark_cast = sV.n
                ev_bank = prep_hand["ev"]
                for bi in range(4):
                    pt = pts_prep[bi]
                    ptb = pt[:].bitcast(BF16)
                    wbank = ev_bank[-4] if len(ev_bank) >= 4 else None

                    @block.tensor
                    def _(t, msl=msl, bi=bi, ptb=ptb, mcst=mark_cast, wb=wbank,
                          first=(first_w and i == 0 and bi == 0)):
                        t.wait_ge(sV.h, mcst)
                        if wb is not None:
                            t.wait_ge(sV.h, wb)
                        if first:
                            t.wait_ge(sA.h, mark_setup_act)
                        for kk in range(8):
                            t.transpose(ptb[:, kk * 128:(kk + 1) * 128],
                                        msl[:, (bi * 8 + kk) * 128:(bi * 8 + kk + 1) * 128], IDB[:])
                        t.matmul(p_s2[0:1, 0:1], ONE[0:1, 0:1], ONE[0:1, 0:1], start=True, stop=True)
                        t.drain().then_inc(sP.h, 1)
                        sP.bump()

                    mark_tp = sP.n

                    @block.vector
                    def _(v, asmv=asmv, bi=bi, ptb=ptb, mtp=mark_tp,
                          wa=(w_asm if bi == 0 else None)):
                        v.wait_ge(sP.h, mtp)
                        if wa is not None:
                            v.wait_ge(wa[0], wa[1])
                        v.tensor_tensor(out=asmv[:, bi * 1024:(bi + 1) * 1024], in0=ptb[:, 0:1024],
                                        in1=ONEB[:, 0:1].broadcast_to([128, 1024]), op=ALU.mult)
                        v.drain().then_inc(sV.h, 1)
                        sV.bump()

                    ev_bank.append(sV.n)

                tp_marks[i] = sP.n
                mark_ev = sV.n
                if pair:
                    @block.gpsimd
                    def _(g, i=i, asmv=asmv, me=mark_ev, scratch=scratch):
                        g.wait_ge(sV.h, me)
                        g.dma_start(out=scratch[2 * i:2 * i + 2].rearrange("m2 p kc mi -> p m2 kc mi"),
                                    in_=asmv.rearrange("p (m2 kc mi) -> p m2 kc mi", m2=2, mi=128)).then_inc(sN.h, 16)
                        sN.bump(16)
                else:
                    @block.gpsimd
                    def _(g, i=i, asmv=asmv, me=mark_ev, scratch=scratch):
                        ms, hh = i // 2, i % 2
                        g.wait_ge(sV.h, me)
                        g.dma_start(out=scratch[ms][:, hh * 32:(hh + 1) * 32, :],
                                    in_=asmv.rearrange("p (kc mi) -> p kc mi", mi=128)).then_inc(sN.h, 16)
                        sN.bump(16)
                st_marks[i] = (sN.h, sN.n)
                st_last[0] = (sN.h, sN.n)
            prep_hand["tp"] = sP.n
            prep_hand["st"] = st_last[0]

        prep_transposed(w_up, up_s, 64, 16, scl=UPS)
        prep_transposed(w_gate, gate_s, 64, 16, scl=GTS)
        prep_transposed(w_down, down_s, 16, 64, scl=DNS)

        if DBG2:
            @block.sync
            def _(s):
                s.wait_ge(sL.h, sL.n)
                s.wait_ge(sA.h, sA.n)
                for nm, scr in [("d_skall", skip_s), ("d_upall", up_s), ("d_gtall", gate_s), ("d_dnall", down_s)]:
                    s.dma_start(out=dbg[nm], in_=scr).then_inc(sS.h, 16)
                    sS.bump(16)

        prep_done = dict(L=sL.n, C=sC.n, P=sP.n, V=sV.n, A=sA.n, M=sM.n, N=sN.n)
        if DBG:
            @block.sync
            def _(s):
                s.wait_ge(sV.h, prep_done["V"])
                s.wait_ge(sA.h, prep_done["A"])
                s.wait_ge(sL.h, prep_done["L"])
                for nm, ap in [("d_h128", H128[:]), ("d_gam", GAM[:]), ("d_ups", UPS[:]),
                               ("d_su1", SU1[:]), ("d_bcb", BCB[:])]:
                    s.dma_start(out=dbg[nm], in_=ap).then_inc(sS.h, 16)
                    sS.bump(16)
                for nm, ap in [("d_up0", up_s[0]), ("d_dn0", down_s[0]), ("d_sk0", skip_s[0])]:
                    s.dma_start(out=dbg[nm], in_=ap).then_inc(sS.h, 16)
                    sS.bump(16)
            dbg_last[0] = sS.n

        store_mark = [0]
        down_done = [sP.n]

        def ln_stats(src3, scr3, scrf, waits, skip_square=False):
            @block.scalar
            def _(sc, waits=waits, skip_square=skip_square):
                for smh, val in waits:
                    sc.wait_ge(smh.h, val)
                if not skip_square:
                    sc.square(scrf[:, 0:8192], src3.rearrange("p c t -> p (c t)"))
                sc.drain().then_inc(sA.h, 1)
                sA.bump()

            m_sq = sA.n

            @block.vector
            def _(v, m=m_sq):
                v.wait_ge(sA.h, m)
                v.tensor_tensor(out=scr3[:, 0:8, :], in0=scr3[:, 0:8, :], in1=scr3[:, 8:16, :], op=ALU.add)
                v.tensor_tensor(out=scr3[:, 0:4, :], in0=scr3[:, 0:4, :], in1=scr3[:, 4:8, :], op=ALU.add)
                v.tensor_tensor(out=scr3[:, 0:2, :], in0=scr3[:, 0:2, :], in1=scr3[:, 2:4, :], op=ALU.add)
                v.tensor_tensor(out=scr3[:, 0:1, :], in0=scr3[:, 0:1, :], in1=scr3[:, 1:2, :], op=ALU.add)
                v.tensor_tensor(out=scr3[:, 8:16, :], in0=src3[:, 0:8, :], in1=src3[:, 8:16, :], op=ALU.add)
                v.tensor_tensor(out=scr3[:, 8:12, :], in0=scr3[:, 8:12, :], in1=scr3[:, 12:16, :], op=ALU.add)
                v.tensor_tensor(out=scr3[:, 8:10, :], in0=scr3[:, 8:10, :], in1=scr3[:, 10:12, :], op=ALU.add)
                v.tensor_tensor(out=scr3[:, 8:9, :], in0=scr3[:, 8:9, :], in1=scr3[:, 9:10, :], op=ALU.add)
                v.drain().then_inc(sV.h, 1)
                sV.bump()

            m_t1 = sV.n

            @block.tensor
            def _(t, m=m_t1):
                t.wait_ge(sV.h, m)
                t.matmul(p_s2[:], ONE[:], scr3[:, 0, :], start=True, stop=True)
                t.matmul(p_s1[:], ONE[:], scr3[:, 8, :], start=True, stop=True)
                t.drain().then_inc(sP.h, 1)
                sP.bump()

            mm_mark = sP.n

            @block.vector
            def _(v, m=mm_mark):
                v.wait_ge(sP.h, m)
                v.tensor_scalar(out=M_, in0=p_s1[:], scalar1=1.0 / D, scalar2=None, op0=ALU.mult)
                v.tensor_scalar(out=V_, in0=p_s2[:], scalar1=1.0 / D, scalar2=None, op0=ALU.mult)
                v.tensor_tensor(out=STG[:], in0=M_, in1=M_, op=ALU.mult)
                v.tensor_tensor(out=V_, in0=V_, in1=STG[:], op=ALU.subtract)
                v.drain().then_inc(sV.h, 1)
                sV.bump()

            m_var = sV.n

            @block.scalar
            def _(sc, m=m_var):
                sc.wait_ge(sV.h, m)
                sc.activation(out=R_, in_=V_, func=AF.Ln, bias=EPS, scale=1.0)
                sc.activation(out=R_, in_=R_, func=AF.Exp, bias=0.0, scale=-0.5)
                sc.drain().then_inc(sA.h, 1)
                sA.bump()

            return sA.n, mm_mark

        ln_prev_mm = [prep_done["P"]]

        for j in range(NT):
            t0 = j * P
            prev_store = store_mark[0]
            zfree = down_done[0]

            @block.sync
            def _(s, j=j, t0=t0, ps=prev_store):
                if ps:
                    s.wait_ge(sS.h, ps)
                if j == 0:
                    s.wait_ge(sL.h, prep_done["L"])
                    s.wait_ge(sA.h, prep_done["A"])
                    s.wait_ge(sP.h, prep_done["P"])
                    s.wait_ge(sV.h, prep_done["V"])
                    if prep_done["M"]:
                        s.wait_ge(sM.h, prep_done["M"])
                    if prep_done["N"]:
                        s.wait_ge(sN.h, prep_done["N"])
                s.dma_start(out=X[:, 8:16, :], in_=xT3[:, 8:16, t0:t0 + P]).then_inc(sL.h, 16)
                sL.bump(16)

            mark_x = sL.n

            @block.scalar
            def _(sc, j=j, t0=t0, ps=prev_store):
                if ps:
                    sc.wait_ge(sS.h, ps)
                if j == 0:
                    sc.wait_ge(sP.h, prep_done["P"])
                sc.dma_start(out=X[:, 0:8, :], in_=xT3[:, 0:8, t0:t0 + P]).then_inc(sM.h, 16)
                sM.bump(16)

            mark_x2 = sM.n

            @block.sync
            def _(s, zf=zfree):
                s.wait_ge(sP.h, zf)
                s.dma_start(out=WIN[:], in_=win_s).then_inc(sL.h, 16)
                sL.bump(16)
                s.dma_start(out=WOU, in_=wout_s).then_inc(sL.h, 16)
                sL.bump(16)

            mark_wz = sL.n

            skp_marks = {}
            for ms in range(2):
                @block.sync
                def _(s, ms=ms, zf=zfree):
                    s.wait_ge(sP.h, zf)
                    s.dma_start(out=SKP[:, ms, :].rearrange("p (kc mi) -> p kc mi", kc=16),
                                in_=skip_s[ms]).then_inc(sL.h, 16)
                    sL.bump(16)
                skp_marks[ms] = sL.n

            m_ln1, mm1 = ln_stats(X, B3, Bf, [(sL, mark_x), (sM, mark_x2), (sP, ln_prev_mm[0])])
            ln_prev_mm[0] = mm1

            if j == 0:
                dbg_tap("d_st1", STT[:], sA, m_ln1)
            @block.vector
            def _(v, m=m_ln1):
                v.wait_ge(sA.h, m)
                dbg_gate(v)
                for c in range(NCD):
                    v.tensor_tensor(out=STG[:], in0=X[:, c, :], in1=M_, op=ALU.subtract)
                    v.tensor_tensor(out=XB[:, c, :], in0=STG[:], in1=R_, op=ALU.mult)
                v.tensor_tensor(out=V_, in0=V_, in1=R_, op=ALU.mult)
                v.tensor_tensor(out=V_, in0=V_, in1=R_, op=ALU.mult)
                v.drain().then_inc(sV.h, 1)
                sV.bump()

            m_xin = sV.n
            if j == 0:
                dbg_tap("d_xin", XB[:], sV, m_xin)

            @block.scalar
            def _(sc, m=m_xin):
                sc.wait_ge(sV.h, m)
                sc.activation(out=R_, in_=V_, func=AF.Ln, bias=EPS, scale=1.0)
                sc.activation(out=R_, in_=R_, func=AF.Exp, bias=0.0, scale=-0.5)
                sc.drain().then_inc(sA.h, 1)
                sA.bump()

            m_r2 = sA.n

            @block.vector
            def _(v, m=m_r2, mw=mark_wz):
                v.wait_ge(sA.h, m)
                v.wait_ge(sL.h, mw)
                for c in range(NCD):
                    v.tensor_tensor(out=STG[:], in0=XB[:, c, :], in1=R_, op=ALU.mult)
                    v.scalar_tensor_tensor(out=UFL[:, c, :], in0=STG[:], scalar=GAM[:, c:c + 1],
                                           in1=BET[:, c:c + 1].broadcast_to([128, P]),
                                           op0=ALU.mult, op1=ALU.add)
                v.drain().then_inc(sV.h, 1)
                sV.bump()

            m_u = sV.n
            if j == 0:
                dbg_tap("d_u", UFL[:], sV, m_u)

            @block.tensor
            def _(t, m=m_u):
                t.wait_ge(sV.h, m)
                for mc, pt in ((0, p_a), (1, p_b), (2, p_c)):
                    for kc in range(NCD):
                        t.matmul(pt[:], WIN[:, kc, mc * 128:(mc + 1) * 128], UFL[:, kc, :],
                                 start=(kc == 0), stop=(kc == NCD - 1))
                t.drain().then_inc(sP.h, 1)
                sP.bump()

            m_gmm = sP.n

            @block.scalar
            def _(sc, m=m_gmm):
                sc.wait_ge(sP.h, m)
                sc.activation(out=AL[:, 0, :], in_=p_a[:], func=AF.Sigmoid, bias=NBIN[:, 0:1], scale=-1.0)
                sc.activation(out=STG[:], in_=p_a[:], func=AF.Sigmoid, bias=BIN[:, 0:1], scale=1.0)
                sc.activation(out=AL[:, 2, :], in_=p_c[:], func=AF.Tanh, bias=BIN[:, 2:3], scale=1.0)
                sc.drain().then_inc(sA.h, 1)
                sA.bump()

            m_gev = sA.n

            @block.vector
            def _(v, m=m_gev):
                v.wait_ge(sA.h, m)
                v.scalar_tensor_tensor(out=AL[:, 1, :], in0=p_b[:], scalar=BIN[:, 1:2],
                                       in1=STG[:], op0=ALU.add, op1=ALU.mult)
                v.tensor_tensor_scan(out=AL[:, 3, :], data0=AL[:, 0, :], data1=AL[:, 1, :],
                                     initial=MSC[:, 0:1], op0=ALU.mult, op1=ALU.add)
                v.tensor_tensor(out=YB, in0=AL[:, 3, :], in1=AL[:, 2, :], op=ALU.mult)
                v.drain().then_inc(sV.h, 1)
                sV.bump()

            m_y = sV.n

            @block.scalar
            def _(sc, m=m_y):
                sc.wait_ge(sV.h, m)
                sc.copy(MSC[:, 0:1], AL[:, 3, P - 1:P])
                sc.drain().then_inc(sA.h, 1)
                sA.bump()

            m_carry = sA.n
            if j == 0:
                dbg_tap("d_al", AL[:], sV, m_y)
            if DBG:
                @block.sync
                def _(s, j=j, m=m_y):
                    s.wait_ge(sV.h, m)
                    s.dma_start(out=dbg["d_car"][:, j:j+1], in_=MSC[:, 0:1], allow_slow_non_contiguous=True).then_inc(sS.h, 16)
                    sS.bump(16)
                    s.dma_start(out=dbg["d_h511"][:, j:j+1], in_=AL[:, 3, P-1:P], allow_slow_non_contiguous=True).then_inc(sS.h, 16)
                    sS.bump(16)
                dbg_last[0] = sS.n

            ao_ev = {}
            for mc in range(NCD):
                pt = p_a if mc % 2 == 0 else p_b

                @block.tensor
                def _(t, mc=mc, pt=pt, m=m_y, mw=mark_wz):
                    if mc == 0:
                        t.wait_ge(sV.h, m)
                        t.wait_ge(sL.h, mw)
                        dbg_gate(t)
                    t.wait_ge(sL.h, skp_marks[mc])
                    if mc >= 2:
                        t.wait_ge(sV.h, ao_ev[mc - 2])
                    t.matmul(pt[:], WOU[:, mc * 128:(mc + 1) * 128], YB, start=True, stop=False)
                    for kc in range(NCD):
                        t.matmul(pt[:], SKP[:, mc % 2, kc * 128:(kc + 1) * 128], XB[:, kc, :],
                                 start=False, stop=(kc == NCD - 1))
                    t.drain().then_inc(sP.h, 1)
                    sP.bump()

                m_ao = sP.n

                @block.vector
                def _(v, mc=mc, pt=pt, m=m_ao):
                    v.wait_ge(sP.h, m)
                    v.scalar_tensor_tensor(out=A3[:, mc, :], in0=pt[:], scalar=BCB[:, mc:mc + 1],
                                           in1=ONE[:, 0:1].broadcast_to([128, P]),
                                           op0=ALU.add, op1=ALU.mult)
                    v.drain().then_inc(sV.h, 1)
                    sV.bump()

                ao_ev[mc] = sV.n

                if mc + 2 < NCD:
                    m_cons = sP.n

                    @block.sync
                    def _(s, mc=mc, m=m_cons):
                        s.wait_ge(sP.h, m)
                        s.dma_start(out=SKP[:, mc % 2, :].rearrange("p (kc mi) -> p kc mi", kc=16),
                                    in_=skip_s[mc + 2]).then_inc(sL.h, 16)
                        sL.bump(16)
                    skp_marks[mc + 2] = sL.n

            m_ao_all = sV.n
            if j == 0:
                dbg_tap("d_ao", A3[:], sV, m_ao_all)

            m_ln3, mm3 = ln_stats(A3, B3, Bf, [(sV, m_ao_all), (sP, ln_prev_mm[0])])
            ln_prev_mm[0] = mm3

            @block.vector
            def _(v, m=m_ln3):
                v.wait_ge(sA.h, m)
                for c in range(NCD):
                    v.tensor_tensor(out=STG[:], in0=A3[:, c, :], in1=M_, op=ALU.subtract)
                    v.tensor_tensor(out=STG[:], in0=STG[:], in1=R_, op=ALU.mult)
                    v.tensor_tensor(out=X[:, c, :], in0=X[:, c, :], in1=STG[:], op=ALU.add)
                v.drain().then_inc(sV.h, 1)
                sV.bump()

            m_res1 = sV.n
            if j == 0:
                dbg_tap("d_x3", X[:], sV, m_res1)

            m_ln4, mm4 = ln_stats(X, B3, Bf, [(sV, m_res1), (sP, ln_prev_mm[0])])
            ln_prev_mm[0] = mm4

            @block.vector
            def _(v, m=m_ln4):
                v.wait_ge(sA.h, m)
                dbg_gate(v)
                for c in range(NCD):
                    v.tensor_tensor(out=STG[:], in0=X[:, c, :], in1=M_, op=ALU.subtract)
                    v.scalar_tensor_tensor(out=B3[:, c, :], in0=STG[:], scalar=SU1[:, c:c + 1],
                                           in1=R_, op0=ALU.mult, op1=ALU.mult)
                for st, (si, di) in enumerate([(B3, A3), (A3, B3), (B3, A3), (A3, XB)]):
                    h = 1 << st
                    ng = 16 // (2 * h)
                    siv = si.rearrange("p c t -> p (c t)").rearrange(
                        "p (g two i) -> p g two i", g=ng, two=2)
                    div = di.rearrange("p c t -> p (c t)").rearrange(
                        "p (g two i) -> p g two i", g=ng, two=2)
                    v.tensor_tensor(out=div[:, :, 0], in0=siv[:, :, 0], in1=siv[:, :, 1], op=ALU.add)
                    v.tensor_tensor(out=div[:, :, 1], in0=siv[:, :, 0], in1=siv[:, :, 1], op=ALU.subtract)
                v.drain().then_inc(sV.h, 1)
                sV.bump()

            m_x1 = sV.n

            f1_ev = {}
            for cb in range(4):
                @block.tensor
                def _(t, cb=cb, m=m_x1):
                    if cb == 0:
                        t.wait_ge(sV.h, m)
                    if cb >= 1:
                        t.wait_ge(sV.h, f1_ev[cb - 1])
                    for i in range(4):
                        t.matmul(pts4[i][:], H128[:], XB[:, cb * 4 + i, :], start=True, stop=True)
                    t.drain().then_inc(sP.h, 1)
                    sP.bump()

                m_f1 = sP.n

                @block.vector
                def _(v, cb=cb, m=m_f1):
                    v.wait_ge(sP.h, m)
                    for i in range(4):
                        v.tensor_tensor(out=XB[:, cb * 4 + i, :], in0=pts4[i][:],
                                        in1=ONE[:, 0:1].broadcast_to([128, P]), op=ALU.mult)
                    v.drain().then_inc(sV.h, 1)
                    sV.bump()

                f1_ev[cb] = sV.n

            m_x1h = sV.n
            if j == 0:
                dbg_tap("d_x1", XB[:], sA, m_x1h)

            up_marks = {}
            for m0 in range(2):
                @block.sync
                def _(s, m0=m0, zf=zfree, j=j):
                    s.wait_ge(sP.h, zf)
                    if j == 0 and prep_done["N"]:
                        s.wait_ge(sN.h, prep_done["N"])
                    s.dma_start(out=WSu[m0].rearrange("p (kc mi) -> p kc mi", kc=16),
                                in_=up_s[m0]).then_inc(sL.h, 16)
                    sL.bump(16)
                    s.dma_start(out=WSg[m0].rearrange("p (kc mi) -> p kc mi", kc=16),
                                in_=gate_s[m0]).then_inc(sL.h, 16)
                    sL.bump(16)
                up_marks[m0] = [(sL.h, sL.n)]

            evu_marks = {}
            z_marks = {}
            for m in range(NMI):
                ubuf, gbuf = WSu[m % 2], WSg[m % 2]
                ptu = p_a if m % 2 == 0 else p_b
                ptg = p_c if m % 2 == 0 else p_d

                @block.tensor
                def _(t, m=m, ubuf=ubuf, gbuf=gbuf, ptu=ptu, ptg=ptg, mh=m_x1h):
                    if m == 0:
                        t.wait_ge(sV.h, mh)
                    for _smh, _sval in up_marks[m]:
                        t.wait_ge(_smh, _sval)
                    if m >= 2:
                        t.wait_ge(sA.h, evu_marks[m - 2])
                        t.wait_ge(sV.h, z_marks[m - 2])
                    for kc in range(NCD):
                        t.matmul(ptu[:], ubuf[:, kc * 128:(kc + 1) * 128], XB[:, kc, :],
                                 start=(kc == 0), stop=(kc == NCD - 1))
                    for kc in range(NCD):
                        t.matmul(ptg[:], gbuf[:, kc * 128:(kc + 1) * 128], XB[:, kc, :],
                                 start=(kc == 0), stop=(kc == NCD - 1))
                    if m >= 2:
                        pH = p_s1 if m % 2 == 0 else p_s2
                        t.matmul(pH[:], H128[:], Z3[:, m - 2, :], start=True, stop=True)
                    t.drain().then_inc(sP.h, 1)
                    sP.bump()

                m_ug = sP.n

                @block.scalar
                def _(sc, m=m, ptg=ptg, mug=m_ug):
                    sc.wait_ge(sP.h, mug)
                    if m >= 2:
                        sc.wait_ge(sV.h, z_marks[m - 2])
                    o = (m % 2) * 1024
                    sc.activation(out=Bf[:, o + 512:o + 1024], in_=ptg[:], func=AF.Silu,
                                  bias=0.0, scale=1.0)
                    if m + 2 < NMI:
                        sc.dma_start(out=WSu[m % 2].rearrange("p (kc mi) -> p kc mi", kc=16),
                                     in_=up_s[m + 2]).then_inc(sM.h, 16)
                        sM.bump(16)
                        sc.dma_start(out=WSg[m % 2].rearrange("p (kc mi) -> p kc mi", kc=16),
                                     in_=gate_s[m + 2]).then_inc(sM.h, 16)
                        sM.bump(16)
                    sc.drain().then_inc(sA.h, 1)
                    sA.bump()

                if m + 2 < NMI:
                    up_marks[m + 2] = [(sM.h, sM.n)]
                evu_marks[m] = sA.n
                if DBG and j == 0 and m == 40:
                    dbg_tap("d_upv", Bf[:, 0:1024], sA, evu_marks[m])
                    dbg_tap("d_ups2", UPS[:], sA, evu_marks[m])

                @block.vector
                def _(v, m=m, ptu=ptu, me=evu_marks[m]):
                    v.wait_ge(sA.h, me)
                    if DBG and j == 0 and m in (40, 41, 42):
                        dbg_gate(v)
                    o = (m % 2) * 1024
                    v.tensor_tensor(out=Z3[:, m, :], in0=ptu[:],
                                    in1=Bf[:, o + 512:o + 1024], op=ALU.mult)
                    if m >= 2:
                        pH = p_s1 if m % 2 == 0 else p_s2
                        v.tensor_tensor(out=Z3[:, m - 2, :], in0=pH[:],
                                        in1=ONE[:, 0:1].broadcast_to([128, P]), op=ALU.mult)
                    v.drain().then_inc(sV.h, 1)
                    sV.bump()

                z_marks[m] = sV.n
                if DBG and j == 0:
                    @block.gpsimd
                    def _(g, m=m, zm=z_marks[m]):
                        g.wait_ge(sV.h, zm)
                        g.dma_start(out=dbg["d_z2"][:, m, :], in_=Z3[:, m, :]).then_inc(sS.h, 16)
                        sS.bump(16)
                    dbg_last[0] = sS.n

            @block.tensor
            def _(t, zl=z_marks[NMI - 1]):
                t.wait_ge(sV.h, zl)
                t.matmul(p_s1[:], H128[:], Z3[:, NMI - 2, :], start=True, stop=True)
                t.matmul(p_s2[:], H128[:], Z3[:, NMI - 1, :], start=True, stop=True)
                t.drain().then_inc(sP.h, 1)
                sP.bump()

            m_hz = sP.n

            @block.vector
            def _(v, m=m_hz):
                v.wait_ge(sP.h, m)
                v.tensor_tensor(out=Z3[:, NMI - 2, :], in0=p_s1[:],
                                in1=ONE[:, 0:1].broadcast_to([128, P]), op=ALU.mult)
                v.tensor_tensor(out=Z3[:, NMI - 1, :], in0=p_s2[:],
                                in1=ONE[:, 0:1].broadcast_to([128, P]), op=ALU.mult)
                v.drain().then_inc(sV.h, 1)
                sV.bump()

            m_z_all = sV.n
            m_ug_all = sP.n

            T1b = A[:, 0:2048].bitcast(BF16).rearrange("p (c t) -> p c t", c=64)
            T2b = A[:, 2048:4096].bitcast(BF16).rearrange("p (c t) -> p c t", c=64)

            @block.vector
            def _(v, m=m_z_all, T1=T1b, T2=T2b):
                v.wait_ge(sV.h, m)
                dbg_gate(v)
                for g in range(P // 64):
                    zg = Z3[:, :, g * 64:(g + 1) * 64]
                    bufs = [zg, T1, T2, T1, T2, T1, zg]
                    for st in range(6):
                        h = 1 << st
                        ng = 64 // (2 * h)
                        si = bufs[st].rearrange("p (g2 two i) t -> p g2 two i t", g2=ng, two=2)
                        di = bufs[st + 1].rearrange("p (g2 two i) t -> p g2 two i t", g2=ng, two=2)
                        v.tensor_tensor(out=di[:, :, 0], in0=si[:, :, 0], in1=si[:, :, 1], op=ALU.add)
                        v.tensor_tensor(out=di[:, :, 1], in0=si[:, :, 0], in1=si[:, :, 1], op=ALU.subtract)
                v.drain().then_inc(sV.h, 1)
                sV.bump()

            m_h64 = sV.n
            if j == 0:
                dbg_tap("d_v", Z3[:], sV, m_h64)

            dn_marks = {}
            for hh in range(2):
                @block.sync
                def _(s, hh=hh, m=m_ug_all, j=j):
                    s.wait_ge(sP.h, m)
                    if j == 0 and prep_done["N"]:
                        s.wait_ge(sN.h, prep_done["N"])
                    s.dma_start(out=WSd[hh].rearrange("p (kc mi) -> p kc mi", kc=32),
                                in_=down_s[0, :, hh * 32:(hh + 1) * 32, :]).then_inc(sL.h, 16)
                    sL.bump(16)
                dn_marks[(0, hh)] = sL.n

            dn_ev = {}
            for mc in range(NCD):
                pt = p_a if mc % 2 == 0 else p_b
                for hh in range(2):
                    @block.tensor
                    def _(t, mc=mc, hh=hh, pt=pt, m=m_h64):
                        if mc == 0 and hh == 0:
                            t.wait_ge(sV.h, m)
                        t.wait_ge(sL.h, dn_marks[(mc, hh)])
                        if mc >= 2 and hh == 0:
                            t.wait_ge(sA.h, dn_ev[mc - 2])
                        for i in range(32):
                            kc = hh * 32 + i
                            t.matmul(pt[:], WSd[hh][:, i * 128:(i + 1) * 128], Z3[:, kc, :],
                                     start=(kc == 0), stop=(kc == 63))
                        t.drain().then_inc(sP.h, 1)
                        sP.bump()

                    m_dn = sP.n
                    nxt = (mc + 1, hh)
                    if nxt[0] < NCD and nxt not in dn_marks:
                        @block.sync
                        def _(s, hh=hh, nxt=nxt, m=m_dn):
                            s.wait_ge(sP.h, m)
                            s.dma_start(out=WSd[hh].rearrange("p (kc mi) -> p kc mi", kc=32),
                                        in_=down_s[nxt[0], :, nxt[1] * 32:(nxt[1] + 1) * 32, :]).then_inc(sL.h, 16)
                            sL.bump(16)
                        dn_marks[nxt] = sL.n

                m_dmm = sP.n

                @block.scalar
                def _(sc, mc=mc, pt=pt, m=m_dmm):
                    sc.wait_ge(sP.h, m)
                    sc.copy(B3[:, mc, :], pt[:])
                    sc.drain().then_inc(sA.h, 1)
                    sA.bump()

                dn_ev[mc] = sA.n

            down_done[0] = sP.n
            m_fo_all = sA.n
            if j == 0:
                dbg_tap("d_fo", B3[:], sA, m_fo_all)

            m_ln5, mm5 = ln_stats(B3, A3, Af, [(sA, m_fo_all), (sP, ln_prev_mm[0])])
            ln_prev_mm[0] = mm5

            @block.vector
            def _(v, m=m_ln5):
                v.wait_ge(sA.h, m)
                for c in range(NCD):
                    v.tensor_tensor(out=STG[:], in0=B3[:, c, :], in1=M_, op=ALU.subtract)
                    v.tensor_tensor(out=STG[:], in0=STG[:], in1=R_, op=ALU.mult)
                    v.tensor_tensor(out=X[:, c, :], in0=X[:, c, :], in1=STG[:], op=ALU.add)
                v.drain().then_inc(sV.h, 1)
                sV.bump()

            m_out = sV.n

            @block.scalar
            def _(sc, t0=t0, m=m_out):
                sc.wait_ge(sV.h, m)
                sc.dma_start(out=oT3[:, 0:8, t0:t0 + P], in_=X[:, 0:8, :]).then_inc(sS.h, 16)
                sS.bump(16)

            @block.sync
            def _(s, t0=t0, m=m_out):
                s.wait_ge(sV.h, m)
                s.dma_start(out=oT3[:, 8:16, t0:t0 + P], in_=X[:, 8:16, :]).then_inc(sL.h, 16)
                sL.bump(16)

            store_mark[0] = sS.n
            store_sync = sL.n

        @block.sync
        def _(s):
            s.wait_ge(sS.h, sS.n)
            s.wait_ge(sL.h, sL.n)

    return nc


_NC_CACHE = [None]


def kernel(**inputs):
    x = np.asarray(inputs["L0_attn_x"])
    in_common = {
        "w_in": np.asarray(inputs["L0_attn_W_in"]),
        "w_out": np.asarray(inputs["L0_attn_W_out"]),
        "w_skip": np.asarray(inputs["L0_attn_W_skip"]),
        "w_up": np.asarray(inputs["L0_ffn_W_up"]),
        "w_gate": np.asarray(inputs["L0_ffn_W_gate"]),
        "w_down": np.asarray(inputs["L0_ffn_W_down"]),
        "v_gamma": np.asarray(inputs["L0_attn_gamma"]),
        "v_beta": np.asarray(inputs["L0_attn_beta"]),
        "v_bin": np.asarray(inputs["L0_attn_b_in"]),
        "v_bout": np.asarray(inputs["L0_attn_b_out"]),
        "v_bskip": np.asarray(inputs["L0_attn_b_skip"]),
        "v_suug": np.asarray(inputs["L0_ffn_SU_ug"]),
        "v_sudp": np.asarray(inputs["L0_ffn_SU_dp"]),
        "v_wsug": np.asarray(inputs["L0_ffn_Wscale_ug"]),
        "v_wsdp": np.asarray(inputs["L0_ffn_Wscale_dp"]),
    }
    if _NC_CACHE[0] is None:
        _NC_CACHE[0] = build_nc()
    nc = _NC_CACHE[0]
    in_maps = []
    for b in range(B):
        m = dict(in_common)
        m["xT"] = np.ascontiguousarray(x[b].T)
        in_maps.append(m)
    res = run_bass_kernel_spmd(nc, in_maps, core_ids=list(range(B)))
    out = np.empty((B, T, D), dtype=np.float32)
    for b in range(B):
        out[b] = np.asarray(res.results[b]["oT"], dtype=np.float32).T
    return out
